# version 13
# baseline (speedup 1.0000x reference)
import contextlib
import numpy as np
import concourse.bass as bass
import concourse.mybir as mybir
from concourse.bass_utils import run_bass_kernel_spmd

F32 = mybir.dt.float32
BF16 = mybir.dt.bfloat16
I32 = mybir.dt.int32
Act = mybir.ActivationFunctionType
Alu = mybir.AluOpType
AxX = mybir.AxisListType.X

C = 1024
G = 8
HCK = 4096
HGK = 32
HFF = 3072
HGF = 24
EPS = 1e-5
NEGBIG = -1.0e30
VIDX = {"w": 0, "u": 1, "rk": 2, "rv": 3, "rr": 4, "crk": 5, "crr": 6,
        "g1": 7, "b1": 8, "g2": 9, "b2": 10}
ENG_SEM = {"sync": "sQ", "gpsimd": "sP", "vector": "sV", "scalar": "sS",
           "tensor": "sT"}


def build_nc(TCORE=4096, TC=512, dbg=()):
    NCH = TCORE // TC
    assert NCH * TC == TCORE
    W1 = TC + 1

    nc = bass.Bass("TRN2", target_bir_lowering=False, debug=False)

    dram = {}
    specs = [
        ("xT", [128, G, TCORE], F32, "ExternalInput"),
        ("xbnd", [128, G], F32, "ExternalInput"),
        ("wk", [G, 128, C], F32, "ExternalInput"),
        ("wv", [G, 128, C], F32, "ExternalInput"),
        ("wr", [G, 128, C], F32, "ExternalInput"),
        ("wcr", [G, 128, C], F32, "ExternalInput"),
        ("wck", [HGK, 128, C], F32, "ExternalInput"),
        ("wcv", [G, 128, HCK], F32, "ExternalInput"),
        ("wg", [HGF, 128, C], I32, "ExternalInput"),
        ("wu", [HGF, 128, C], I32, "ExternalInput"),
        ("wd", [G, 128, HFF], I32, "ExternalInput"),
        ("vcs", [128, 11, G], F32, "ExternalInput"),
        ("cfg", [128, 4], F32, "ExternalInput"),
        ("outT", [128, G, TCORE], F32, "ExternalOutput"),
        ("sp_ekw", [128, G, TCORE], F32, None),
        ("sp_zv", [128, G, TCORE], F32, None),
        ("sp_sgr", [128, G, TCORE], F32, None),
        ("sp_x2", [128, G, TCORE], F32, None),
        ("wgbf", [HGF, 128, C], BF16, None),
        ("wubf", [HGF, 128, C], BF16, None),
        ("wdbf", [G, 128, HFF], BF16, None),
        ("sp_rows", [2, TCORE], F32, None),
        ("ccin", [128, 24], F32, None),
        ("ccout", [2, 128, 24], F32, None),
    ]
    for name, shape, dt, kind in specs:
        if kind is None:
            dram[name] = nc.dram_tensor(name, shape, dt).ap()
        else:
            dram[name] = nc.dram_tensor(name, shape, dt, kind=kind).ap()
    for name in dbg:
        dram["dbg_" + name] = nc.dram_tensor(
            "dbg_" + name, [128, G, TCORE], F32, kind="ExternalOutput").ap()

    est = contextlib.ExitStack()

    def sbuf(name, shape, dt=F32):
        return est.enter_context(nc.sbuf_tensor(name, shape, dt))

    with est:
        ARN = ["aE", "aZ", "aR", "aA", "aB", "aP1", "aP2", "aX", "aY", "aXB"]
        AR = {n: sbuf(n, [128, G * W1]) for n in ARN}
        X2ALT = sbuf("x2alt", [128, G * W1])
        ring_a = sbuf("ring_a", [128, 2 * 1024])
        ring_b = sbuf("ring_b", [128, 4 * 1024], BF16)
        vcs_sb = sbuf("vcs_sb", [128, 11 * G])
        cfg_sb = sbuf("cfg_sb", [128, 4])
        COLS = {n: sbuf(n, [128, G]) for n in (
            "ew8", "emw8", "euw8", "rkm8", "rvm8", "rrm8", "crkm8", "crrm8",
            "bias8",
            "tc1", "tc2", "tc3", "tc4", "ca8", "cb8", "f18", "a0c", "p0c",
            "x1l8", "bnd_xa", "bnd_xb", "ylst", "am8", "bm8")}
        a_st = [sbuf(f"a_st{i}", [128, G]) for i in range(2)]
        b_st = [sbuf(f"b_st{i}", [128, G]) for i in range(2)]
        p_st = [sbuf(f"p_st{i}", [128, G]) for i in range(2)]
        a_sv = sbuf("a_sv", [128, NCH * G])
        b_sv = sbuf("b_sv", [128, NCH * G])
        p_sv = sbuf("p_sv", [128, NCH * G])
        negb = sbuf("negb", [128, 1])
        eps1 = sbuf("eps1", [1, 1])
        onec = sbuf("onec", [128, 1])
        mask1 = sbuf("mask1", [128, 1])
        ccs = sbuf("ccs", [128, 24])
        ROWB = sbuf("rowb", [1, 5 * TC])
        RSLOT = {"row_s": 0, "row_mu": 0, "row_q": 1, "row_var": 1,
                 "row_t": 2, "row_mr": 2, "row_r0": 3, "row_rho": 4}

        def rowap(name, n=TC):
            s = RSLOT[name]
            return ROWB[0:1, s * TC:s * TC + n]

        def rown(name):
            return f"rows{RSLOT[name]}"

        PB = [est.enter_context(nc.psum_tensor(f"pb{i}", [128, TC], F32))
              for i in range(8)]

        EMIT = {k: [] for k in ("sync", "vector", "scalar", "tensor",
                                "gpsimd")}
        sems = {}
        DMA_ROT = 4
        sem_names = ["sV", "sS", "sT", "sP", "sCC"]
        sem_names += [f"sQ{i}" for i in range(DMA_ROT)]
        sem_names += [f"sG{i}" for i in range(DMA_ROT)]
        for s in sem_names:
            sems[s] = {"h": est.enter_context(nc.semaphore(s)), "n": 0}
        dma_idx = {"sync": 0, "gpsimd": 0}
        DMA_BASE = {"sync": "sQ", "gpsimd": "sG"}
        last_wait = {e: {} for e in EMIT}
        trk_w = {}
        trk_r = {}
        pend_w = {e: set() for e in EMIT}
        pend_r = {e: set() for e in EMIT}
        dirty = {e: set() for e in EMIT}
        dirty_r = {e: set() for e in EMIT}

        def _wait(eng, sem, val):
            if val is None or val <= 0:
                return
            if last_wait[eng].get(sem, 0) >= val:
                return
            last_wait[eng][sem] = val
            h = sems[sem]["h"]
            EMIT[eng].append(lambda e, h=h, v=val: e.wait_ge(h, v))

        def sig(eng):
            s = ENG_SEM[eng]
            if eng == "sync":
                pend_w[eng].clear()
                pend_r[eng].clear()
                dirty[eng].clear()
                return 0
            if not pend_w[eng] and not pend_r[eng]:
                return sems[s]["n"]
            sems[s]["n"] += 1
            n = sems[s]["n"]
            h = sems[s]["h"]
            EMIT[eng].append(lambda e, h=h: e.drain().then_inc(h, 1))
            for b in pend_w[eng]:
                trk_w[b] = (s, n)
            for b in pend_r[eng]:
                trk_r.setdefault(b, {})[s] = n
            pend_w[eng].clear()
            pend_r[eng].clear()
            dirty[eng].clear()
            dirty_r[eng].clear()
            return n

        def use(eng, bufs_r, bufs_w, dma_issue=False):
            own = ENG_SEM[eng]

            def is_own(sname):
                if sname == own:
                    return True
                if dma_issue and eng in DMA_BASE:
                    return sname.startswith(DMA_BASE[eng])
                return False
            need_drain = False
            for b in bufs_r:
                if b in dirty[eng]:
                    need_drain = True
                else:
                    for e2 in EMIT:
                        if e2 != eng and b in pend_w[e2]:
                            sig(e2)
            for b in bufs_w:
                if b in dirty_r[eng] or b in dirty[eng]:
                    need_drain = True
                for e2 in EMIT:
                    if e2 != eng and (b in pend_w[e2] or b in pend_r[e2]):
                        sig(e2)
            if need_drain:
                EMIT[eng].append(lambda e: e.drain())
                dirty[eng].clear()
                dirty_r[eng].clear()
            for b in bufs_r:
                w = trk_w.get(b)
                if w and not is_own(w[0]):
                    _wait(eng, w[0], w[1])
            for b in bufs_w:
                w = trk_w.get(b)
                if w and not is_own(w[0]):
                    _wait(eng, w[0], w[1])
                for s2, n2 in list(trk_r.get(b, {}).items()):
                    if not is_own(s2):
                        _wait(eng, s2, n2)

        def op(eng, fn, rd=(), wr=()):
            use(eng, bufs_r=rd, bufs_w=wr)
            EMIT[eng].append(fn)
            for b in wr:
                pend_w[eng].add(b)
                dirty[eng].add(b)
                trk_r.pop(b, None)
            for b in rd:
                pend_r[eng].add(b)
                dirty_r[eng].add(b)

        def dma(q, out_ap, in_ap, rd=(), wr=()):
            use(q, bufs_r=rd, bufs_w=wr, dma_issue=True)
            s = DMA_BASE[q] + str(dma_idx[q] % DMA_ROT)
            dma_idx[q] += 1
            _wait(q, s, sems[s]["n"])
            sems[s]["n"] += 16
            n = sems[s]["n"]
            h = sems[s]["h"]
            EMIT[q].append(lambda e, o=out_ap, i=in_ap, h=h:
                           e.dma_start(out=o, in_=i).then_inc(h, 16))
            for b in wr:
                trk_w[b] = (s, n)
                trk_r.pop(b, None)
            for b in rd:
                trk_r.setdefault(b, {})[s] = n
            return (s, n)

        def barrier():
            for e in EMIT:
                sig(e)
            for e in EMIT:
                for s in sems:
                    if s != "sCC":
                        _wait(e, s, sems[s]["n"])

        def c3(ar):
            return ar[:, 0:G * TC].rearrange("p (g t) -> p g t", g=G)

        def cg(ar, g):
            return ar[:, g * TC:(g + 1) * TC]

        def sb_body(ar):
            return ar[:, :].rearrange("p (g t) -> p g t", g=G)[:, :, 1:W1]

        def sb_shift(ar):
            return ar[:, :].rearrange("p (g t) -> p g t", g=G)[:, :, 0:TC]

        def gb(ar, g, lo, hi):
            return ar[:, g * W1 + lo:g * W1 + hi]

        def vc8(name):
            i = VIDX[name]
            return vcs_sb[:, i * G:(i + 1) * G]

        def col(t, i):
            return t[:, i:i + 1]

        def bc(colap, n=TC):
            return colap.to_broadcast((128, n))

        ARN_BY_ID = {}
        E, Z, R = AR["aE"], AR["aZ"], AR["aR"]
        A, B = AR["aA"], AR["aB"]
        P1, P2 = AR["aP1"], AR["aP2"]
        X, Y, XB = AR["aX"], AR["aY"], AR["aXB"]
        for k, v in AR.items():
            ARN_BY_ID[id(v)] = k
        ARN_BY_ID[id(X2ALT)] = "x2alt"

        def XOF(j):
            return X if j % 2 == 0 else X2ALT

        def arn(ar):
            return ARN_BY_ID[id(ar)]

        dma("sync", vcs_sb[:, :], dram["vcs"].rearrange("p v g -> p (v g)"),
            wr=["vcs"])
        dma("sync", cfg_sb[:, :], dram["cfg"][:, :], wr=["cfg"])
        dma("sync", E[:, 0:G], dram["xbnd"][:, :], wr=["aE"])

        op("scalar", lambda e: e.activation(COLS["ew8"][:, :], vc8("w"),
                                            Act.Exp), rd=["vcs"], wr=["ew8"])
        op("scalar", lambda e: e.activation(COLS["emw8"][:, :], vc8("w"),
                                            Act.Exp, scale=-1.0),
           rd=["vcs"], wr=["emw8"])
        op("vector", lambda e: e.tensor_sub(COLS["tc1"][:, :], vc8("u"),
                                            vc8("w")), rd=["vcs"], wr=["tc1"])
        op("scalar", lambda e: e.activation(COLS["euw8"][:, :],
                                            COLS["tc1"][:, :], Act.Exp),
           rd=["tc1"], wr=["euw8"])
        op("vector", lambda e: e.memset(negb[:, :], NEGBIG), wr=["negb"])
        op("vector", lambda e: e.memset(eps1[:, :], EPS), wr=["eps1"])
        op("vector", lambda e: e.memset(onec[:, :], 1.0), wr=["onec"])
        op("vector", lambda e: e.memset(a_st[0][:, :], 0.0), wr=["a_st0"])
        op("vector", lambda e: e.memset(b_st[0][:, :], 0.0), wr=["b_st0"])
        op("vector", lambda e: e.memset(p_st[0][:, :], 0.0), wr=["p_st0"])
        op("vector", lambda e: e.tensor_copy(mask1[:, :], cfg_sb[:, 0:1]),
           rd=["cfg"], wr=["mask1"])
        for nm, src in (("rkm8", "rk"), ("rvm8", "rv"), ("rrm8", "rr"),
                        ("crkm8", "crk"), ("crrm8", "crr")):
            op("vector", lambda e, nm=nm, src=src: e.tensor_scalar(
                COLS[nm][:, :], vc8(src), 1.0, -1.0, Alu.mult, Alu.add),
               rd=["vcs"], wr=[nm])

        def row_lnstats(simple, n=TC):
            op("vector", lambda e: e.tensor_scalar(
                rowap("row_mu", n), rowap("row_s", n), 1.0 / C, None,
                Alu.mult), rd=[rown("row_s")], wr=[rown("row_mu")])
            op("vector", lambda e: e.tensor_scalar(
                rowap("row_var", n), rowap("row_q", n), 1.0 / C, None,
                Alu.mult), rd=[rown("row_q")], wr=[rown("row_var")])
            op("vector", lambda e: e.tensor_tensor(
                rowap("row_t", n), rowap("row_mu", n), rowap("row_mu", n),
                Alu.mult), rd=[rown("row_mu")], wr=[rown("row_t")])
            op("vector", lambda e: e.tensor_sub(
                rowap("row_var", n), rowap("row_var", n), rowap("row_t", n)),
               rd=[rown("row_var"), rown("row_t")], wr=[rown("row_var")])
            op("scalar", lambda e: e.activation(
                rowap("row_t", n), rowap("row_var", n), Act.Ln,
                bias=eps1[0:1, 0:1]),
               rd=[rown("row_var"), "eps1"], wr=[rown("row_t")])
            op("scalar", lambda e: e.activation(
                rowap("row_r0", n), rowap("row_t", n), Act.Exp, scale=-0.5),
               rd=[rown("row_t")], wr=[rown("row_r0")])
            if simple:
                return
            op("scalar", lambda e: e.activation(
                rowap("row_rho", n), rowap("row_t", n), Act.Exp, scale=-1.0),
               rd=[rown("row_t")], wr=[rown("row_rho")])
            op("vector", lambda e: e.tensor_tensor(
                rowap("row_rho", n), rowap("row_var", n),
                rowap("row_rho", n), Alu.mult),
               rd=[rown("row_var"), rown("row_rho")], wr=[rown("row_rho")])
            op("scalar", lambda e: e.activation(
                rowap("row_t", n), rowap("row_rho", n), Act.Ln,
                bias=eps1[0:1, 0:1]),
               rd=[rown("row_rho"), "eps1"], wr=[rown("row_t")])
            op("scalar", lambda e: e.activation(
                rowap("row_rho", n), rowap("row_t", n), Act.Exp, scale=-0.5),
               rd=[rown("row_t")], wr=[rown("row_rho")])
            op("vector", lambda e: e.tensor_tensor(
                rowap("row_rho", n), rowap("row_r0", n), rowap("row_rho", n),
                Alu.mult),
               rd=[rown("row_r0"), rown("row_rho")], wr=[rown("row_rho")])

        def mk_murho(rho_name, n=TC):
            op("vector", lambda e: e.tensor_tensor(
                rowap("row_mr", n), rowap("row_mu", n), rowap(rho_name, n),
                Alu.mult),
               rd=[rown("row_mu"), rown(rho_name)], wr=[rown("row_mr")])

        def bcasts(r1, r2, bi=4, bj=5):
            op("tensor", lambda e, bi=bi: e.matmul(
                PB[bi][:, :], onec[0:1, 0:1].to_broadcast((1, 128)),
                rowap(r1), start=True, stop=True),
               rd=[rown(r1), "onec"], wr=[f"pb{bi}"])
            op("tensor", lambda e, bj=bj: e.matmul(
                PB[bj][:, :], onec[0:1, 0:1].to_broadcast((1, 128)),
                rowap(r2), start=True, stop=True),
               rd=[rown(r2), "onec"], wr=[f"pb{bj}"])

        def bcast_scalar(rname, off, pbi):
            op("tensor", lambda e: e.matmul(
                PB[pbi][:, 0:1], onec[0:1, 0:1].to_broadcast((1, 128)),
                ROWB[0:1, RSLOT[rname] * TC + off:RSLOT[rname] * TC + off + 1],
                start=True, stop=True),
               rd=[rown(rname), "onec"], wr=[f"pb{pbi}"])

        def ln_stats(src_ar, body=False, sqar=None):
            sq = A if sqar is None else sqar
            srcv = sb_body(src_ar) if body else c3(src_ar)
            op("scalar", lambda e: e.activation(c3(sq), srcv, Act.Square),
               rd=[arn(src_ar)], wr=[arn(sq)])
            for g in range(G):
                s_ap = gb(src_ar, g, 1, W1) if body else cg(src_ar, g)
                op("tensor", lambda e, s=s_ap, g=g: e.matmul(
                    PB[6][0:1, :], onec[:, :], s, start=(g == 0),
                    stop=(g == 7)), rd=[arn(src_ar), "onec"], wr=["pb6"])
            for g in range(G):
                op("tensor", lambda e, g=g: e.matmul(
                    PB[7][0:1, :], onec[:, :], cg(sq, g), start=(g == 0),
                    stop=(g == 7)), rd=[arn(sq), "onec"], wr=["pb7"])
            op("vector", lambda e: e.tensor_copy(rowap("row_s"),
                                                 PB[6][0:1, :]),
               rd=["pb6"], wr=[rown("row_s")])
            op("vector", lambda e: e.tensor_copy(rowap("row_q"),
                                                 PB[7][0:1, :]),
               rd=["pb7"], wr=[rown("row_q")])

        def apply_ln(src_ar, dst_ap_nm, affine, src_body=False,
                     bi=4, bj=5):
            for g in range(G):
                s_ap = gb(src_ar, g, 1, W1) if src_body else cg(src_ar, g)
                d_ap, d_nm = dst_ap_nm(g)
                op("vector", lambda e, s=s_ap, d=d_ap, bi=bi: e.tensor_tensor(
                    d, s, PB[bi][:, :], Alu.mult),
                   rd=[arn(src_ar), f"pb{bi}"], wr=[d_nm])
                op("vector", lambda e, d=d_ap, bj=bj: e.tensor_sub(
                    d, d, PB[bj][:, :]), rd=[d_nm, f"pb{bj}"], wr=[d_nm])
                if affine:
                    gn, bn = affine
                    op("vector", lambda e, d=d_ap, g=g: e.tensor_scalar(
                        d, d, col(vc8(gn), g), col(vc8(bn), g),
                        Alu.mult, Alu.add), rd=[d_nm, "vcs"], wr=[d_nm])

        op("scalar", lambda e: e.square(Z[:, 0:G], E[:, 0:G]),
           rd=["aE"], wr=["aZ"])
        op("tensor", lambda e: e.matmul(PB[6][0:1, 0:G], onec[:, :],
                                        E[:, 0:G], start=True, stop=True),
           rd=["onec", "aE"], wr=["pb6"])
        op("tensor", lambda e: e.matmul(PB[7][0:1, 0:G], onec[:, :],
                                        Z[:, 0:G], start=True, stop=True),
           rd=["onec", "aZ"], wr=["pb7"])
        op("vector", lambda e: e.tensor_reduce(
            rowap("row_s", 1), PB[6][0:1, 0:G], AxX, Alu.add),
           rd=["pb6"], wr=[rown("row_s")])
        op("vector", lambda e: e.tensor_reduce(
            rowap("row_q", 1), PB[7][0:1, 0:G], AxX, Alu.add),
           rd=["pb7"], wr=[rown("row_q")])
        row_lnstats(simple=False, n=1)
        bcast_scalar("row_mu", 0, 4)
        bcast_scalar("row_rho", 0, 5)
        op("vector", lambda e: e.tensor_sub(
            Z[:, 0:G], E[:, 0:G], bc(PB[4][:, 0:1], G)),
           rd=["aE", "pb4"], wr=["aZ"])
        op("vector", lambda e: e.tensor_tensor(
            Z[:, 0:G], Z[:, 0:G], bc(PB[5][:, 0:1], G), Alu.mult),
           rd=["aZ", "pb5"], wr=["aZ"])
        op("vector", lambda e: e.tensor_tensor(
            Z[:, 0:G], Z[:, 0:G], vc8("g1"), Alu.mult),
           rd=["aZ", "vcs"], wr=["aZ"])
        op("vector", lambda e: e.tensor_add(Z[:, 0:G], Z[:, 0:G], vc8("b1")),
           rd=["aZ", "vcs"], wr=["aZ"])
        op("vector", lambda e: e.tensor_tensor(
            COLS["bnd_xa"][:, :], Z[:, 0:G], bc(mask1[:, :], G), Alu.mult),
           rd=["aZ", "mask1"], wr=["bnd_xa"])

        par = 0
        for nm_i, nm_o, ngrp, wid in (("wg", "wgbf", HGF, C),
                                      ("wu", "wubf", HGF, C),
                                      ("wd", "wdbf", G, HFF)):
            for t in range(ngrp):
                for pp in range(wid // 1024):
                    src, sname = (X, "aX") if par == 0 else (P2, "aP2")
                    dst, dname = (Y, "aY") if par == 0 else (XB, "aXB")
                    par ^= 1
                    i32v = src[:, 0:1024].bitcast(I32)
                    bfv = dst[:, 0:512].bitcast(BF16)[:, 0:1024]
                    dma("sync", i32v,
                        dram[nm_i][t][:, pp * 1024:(pp + 1) * 1024],
                        wr=[sname])
                    op("vector", lambda e, o=bfv, i=i32v: e.tensor_copy(o, i),
                       rd=[sname], wr=[dname])
                    sig("vector")
                    dma("gpsimd",
                        dram[nm_o][t][:, pp * 1024:(pp + 1) * 1024], bfv,
                        rd=[dname])

        def emit_wkv_chunk(j, Abuf, Bbuf, a_init, a_init_nm, b_init,
                           b_init_nm, a_out, a_out_nm, p_old, p_old_nm,
                           p_new, p_new_nm, b_out, b_out_nm, per_dg_bufs,
                           save_mini):
            for dg in range(G):
                aS, aB, a0ap, aE_ = Abuf(dg)
                bS, bB, b0ap, bE_ = Bbuf(dg)
                mx_ap, gn_ap, mx_nm, gn_nm = per_dg_bufs(dg)
                ai = a_init(dg)
                bi = b_init(dg)
                op("vector", lambda e, b0ap=b0ap, bi=bi: e.tensor_copy(
                    b0ap, bi), rd=[b_init_nm], wr=["aB"])
                op("vector", lambda e, a0ap=a0ap, ai=ai: e.tensor_copy(
                    a0ap, ai), rd=[a_init_nm], wr=["aA"])
                op("vector", lambda e, dg=dg, bB=bB, bi=bi:
                   e.tensor_tensor_scan(
                    bB, bc(col(COLS["ew8"], dg)), cg(E, dg), bi,
                    Alu.mult, Alu.add),
                   rd=["ew8", "aE", b_init_nm], wr=["aB"])
                op("vector", lambda e, dg=dg, mx_ap=mx_ap, bS=bS:
                   e.scalar_tensor_tensor(
                    mx_ap, cg(E, dg), col(COLS["emw8"], dg), bS,
                    Alu.mult, Alu.max),
                   rd=["aE", "emw8", "aB"], wr=[mx_nm])
                op("scalar", lambda e, mx_ap=mx_ap, gn_ap=gn_ap:
                   e.activation(gn_ap, mx_ap, Act.Ln),
                   rd=[mx_nm], wr=[gn_nm])
                op("scalar", lambda e, gn_ap=gn_ap: e.activation(
                    gn_ap, gn_ap, Act.Exp, scale=-1.0),
                   rd=[gn_nm], wr=[gn_nm])
                op("vector", lambda e, mx_ap=mx_ap, gn_ap=gn_ap, bB=bB:
                   e.tensor_tensor(mx_ap, bB, gn_ap, Alu.mult),
                   rd=["aB", gn_nm], wr=[mx_nm])
                op("vector", lambda e, dg=dg, mx_ap=mx_ap, gn_ap=gn_ap:
                   e.scalar_tensor_tensor(
                    gn_ap, mx_ap, col(COLS["emw8"], dg), cg(Z, dg),
                    Alu.mult, Alu.mult),
                   rd=[mx_nm, "emw8", "aZ"], wr=[gn_nm])
                op("vector", lambda e, aB=aB, mx_ap=mx_ap, gn_ap=gn_ap,
                   ai=ai: e.tensor_tensor_scan(
                    aB, mx_ap, gn_ap, ai, Alu.mult, Alu.add),
                   rd=[gn_nm, mx_nm, a_init_nm], wr=["aA"])
                if save_mini:
                    op("vector", lambda e, dg=dg, aS=aS: e.tensor_copy(
                        col(COLS["am8"], dg), aS[:, TC - 1:TC]),
                       rd=["aA"], wr=["am8"])
                    op("vector", lambda e, dg=dg, bS=bS: e.tensor_copy(
                        col(COLS["bm8"], dg), bS[:, TC - 1:TC]),
                       rd=["aB"], wr=["bm8"])
                op("vector", lambda e, dg=dg, bE_=bE_: e.reciprocal(
                    col(COLS["tc1"], dg), bE_), rd=["aB"], wr=["tc1"])
                op("vector", lambda e, dg=dg, aE_=aE_, ao=a_out(dg):
                   e.tensor_tensor(ao, aE_, col(COLS["tc1"], dg), Alu.mult),
                   rd=["aA", "tc1"], wr=[a_out_nm])
                op("scalar", lambda e, dg=dg, bE_=bE_: e.activation(
                    col(COLS["tc2"], dg), bE_, Act.Ln),
                   rd=["aB"], wr=["tc2"])
                op("vector", lambda e, dg=dg, po=p_new(dg), pi=p_old(dg):
                   e.tensor_add(po, pi, col(COLS["tc2"], dg)),
                   rd=[p_old_nm, "tc2"], wr=[p_new_nm])
            op("vector", lambda e, t=b_out: e.memset(t[:, :], 1.0),
               wr=[b_out_nm])

        barrier()


        def xload1(j):
            dma("sync", c3(XOF(j)), dram["xT"][:, :, j * TC:(j + 1) * TC],
                wr=[arn(XOF(j))])

        def V1(j):
            Xj = XOF(j)
            if j + 1 < NCH:
                xload1(j + 1)
            ln_stats(Xj)
            row_lnstats(simple=False)
            dma("gpsimd", dram["sp_rows"][0:1, j * TC:(j + 1) * TC],
                rowap("row_mu"), rd=[rown("row_mu")])
            dma("gpsimd", dram["sp_rows"][1:2, j * TC:(j + 1) * TC],
                rowap("row_r0"), rd=[rown("row_r0")])
            mk_murho("row_rho")
            bcasts("row_rho", "row_mr")
            op("vector", lambda e: e.tensor_copy(
                XB[:, 0:G * W1:W1], COLS["bnd_xa"][:, :]),
               rd=["bnd_xa"], wr=["aXB"])
            apply_ln(Xj, lambda g: (gb(XB, g, 1, W1), "aXB"),
                     affine=("g1", "b1"))
            op("vector", lambda e: e.tensor_copy(
                COLS["bnd_xa"][:, :], XB[:, TC:G * W1:W1]),
               rd=["aXB"], wr=["bnd_xa"])
        def V1b(j):
            op("vector", lambda e: e.tensor_sub(c3(P2), sb_body(XB),
                                                sb_shift(XB)),
               rd=["aXB"], wr=["aP2"])
            for g in range(G):
                op("vector", lambda e, g=g: e.scalar_tensor_tensor(
                    cg(Y, g), cg(P2, g), col(COLS["rkm8"], g),
                    gb(XB, g, 1, W1), Alu.mult, Alu.add),
                   rd=["aP2", "rkm8", "aXB"], wr=["aY"])
            for g in range(G):
                op("vector", lambda e, g=g: e.scalar_tensor_tensor(
                    cg(P1, g), cg(P2, g), col(COLS["rvm8"], g),
                    gb(XB, g, 1, W1), Alu.mult, Alu.add),
                   rd=["aP2", "rvm8", "aXB"], wr=["aP1"])
            for g in range(G):
                op("vector", lambda e, g=g: e.scalar_tensor_tensor(
                    cg(P2, g), cg(P2, g), col(COLS["rrm8"], g),
                    gb(XB, g, 1, W1), Alu.mult, Alu.add),
                   rd=["aP2", "rrm8", "aXB"], wr=["aP2"])

        def T1(j):
            for mixrole, mnm, wnm, dst, dstn, act in (
                    ("P2", "aP2", "wr", R, "aR", Act.Sigmoid),
                    ("Y", "aY", "wk", E, "aE", None),
                    ("P1", "aP1", "wv", Z, "aZ", None)):
                Mt = {"Y": Y, "P1": P1, "P2": P2}[mixrole]
                for dg in range(G):
                    slot = dg % 2
                    sl_ap = ring_a[:, slot * 1024:(slot + 1) * 1024]
                    dma("sync", sl_ap, dram[wnm][dg], wr=[f"ring{slot}"])
                    pb_i = dg % 4
                    for g in range(G):
                        op("tensor", lambda e, dg=dg, g=g, sl=sl_ap,
                           ms=Mt, pb_i=pb_i, mnm=mnm: e.matmul(
                               PB[pb_i][:, :],
                               sl[:, g * 128:(g + 1) * 128], cg(ms, g),
                               start=(g == 0), stop=(g == 7)),
                           rd=[f"ring{slot}", mnm], wr=[f"pb{pb_i}"])
                    if act is None:
                        op("scalar", lambda e, dg=dg, d=dst, pb_i=pb_i:
                           e.copy(cg(d, dg), PB[pb_i][:, :]),
                           rd=[f"pb{pb_i}"], wr=[dstn])
                    else:
                        op("scalar", lambda e, dg=dg, d=dst, pb_i=pb_i,
                           act=act: e.activation(cg(d, dg), PB[pb_i][:, :],
                                                 act),
                           rd=[f"pb{pb_i}"], wr=[dstn])

        def W1ph(j):
            pj, pn = j % 2, (j + 1) % 2
            Xj = XOF(j)
            op("vector", lambda e, s=p_st[pj], j=j: e.tensor_copy(
                p_sv[:, j * G:(j + 1) * G], s[:, :]),
               rd=[f"p_st{pj}"], wr=["p_sv"])
            op("vector", lambda e, p=p_st[pj]: e.tensor_sub(
                COLS["bias8"][:, :], vc8("w"), p[:, :]),
               rd=["vcs", f"p_st{pj}"], wr=["bias8"])
            for dg in range(G):
                op("scalar", lambda e, dg=dg: e.activation(
                    cg(E, dg), cg(E, dg), Act.Exp,
                    bias=col(COLS["bias8"], dg)),
                   rd=["aE", "bias8"], wr=["aE"])
            for dg in range(G):
                op("vector", lambda e, dg=dg: e.tensor_tensor(
                    cg(Z, dg), cg(E, dg), cg(Z, dg), Alu.mult),
                   rd=["aE", "aZ"], wr=["aZ"])
            emit_wkv_chunk(
                j,
                Abuf=lambda dg: (A[:, 0:TC], A[:, 1:W1], A[:, 0:1],
                                 A[:, TC:W1]),
                Bbuf=lambda dg: (B[:, 0:TC], B[:, 1:W1], B[:, 0:1],
                                 B[:, TC:W1]),
                a_init=lambda dg, s=a_st[pj]: col(s, dg),
                a_init_nm=f"a_st{pj}",
                b_init=lambda dg, s=b_st[pj]: col(s, dg),
                b_init_nm=f"b_st{pj}",
                a_out=lambda dg, s=a_st[pn]: col(s, dg),
                a_out_nm=f"a_st{pn}",
                p_old=lambda dg, s=p_st[pj]: col(s, dg),
                p_old_nm=f"p_st{pj}",
                p_new=lambda dg, s=p_st[pn]: col(s, dg),
                p_new_nm=f"p_st{pn}",
                b_out=b_st[pn], b_out_nm=f"b_st{pn}",
                per_dg_bufs=lambda dg: (A[:, W1:W1 + TC],
                                        A[:, 2 * W1:2 * W1 + TC],
                                        "aA", "aA"),
                save_mini=(j == NCH - 1))

            sig("vector")
            sig("scalar")
            dma("gpsimd", dram["sp_ekw"][:, :, j * TC:(j + 1) * TC], c3(E),
                rd=["aE"])
            dma("gpsimd", dram["sp_zv"][:, :, j * TC:(j + 1) * TC], c3(Z),
                rd=["aZ"])
            dma("gpsimd", dram["sp_sgr"][:, :, j * TC:(j + 1) * TC], c3(R),
                rd=["aR"])

            if j == NCH - 1:
                zl = Z[:, TC - 1:G * TC:TC]
                el = E[:, TC - 1:G * TC:TC]
                xl = Xj[:, TC - 1:G * TC:TC]
                rl = R[:, TC - 1:G * TC:TC]
                op("vector", lambda e, zl=zl: e.tensor_tensor(
                    COLS["tc3"][:, :], zl, COLS["euw8"][:, :], Alu.mult),
                   rd=["aZ", "euw8"], wr=["tc3"])
                op("vector", lambda e: e.tensor_add(
                    COLS["tc3"][:, :], COLS["tc3"][:, :],
                    COLS["am8"][:, :]), rd=["tc3", "am8"], wr=["tc3"])
                op("vector", lambda e, el=el: e.tensor_tensor(
                    COLS["tc4"][:, :], el, COLS["euw8"][:, :], Alu.mult),
                   rd=["aE", "euw8"], wr=["tc4"])
                op("vector", lambda e: e.tensor_add(
                    COLS["tc4"][:, :], COLS["tc4"][:, :],
                    COLS["bm8"][:, :]), rd=["tc4", "bm8"], wr=["tc4"])
                op("vector", lambda e: e.reciprocal(
                    COLS["tc1"][:, 0:G], COLS["tc4"][:, :]),
                   rd=["tc4"], wr=["tc1"])
                op("vector", lambda e: e.tensor_tensor(
                    COLS["ylst"][:, :], COLS["tc3"][:, :],
                    COLS["tc1"][:, 0:G], Alu.mult),
                   rd=["tc3", "tc1"], wr=["ylst"])
                bcast_scalar("row_mu", TC - 1, 4)
                bcast_scalar("row_r0", TC - 1, 5)
                op("vector", lambda e, xl=xl: e.tensor_sub(
                    COLS["tc2"][:, :], xl, bc(PB[4][:, 0:1], G)),
                   rd=[arn(Xj), "pb4"], wr=["tc2"])
                op("vector", lambda e: e.tensor_tensor(
                    COLS["tc2"][:, :], COLS["tc2"][:, :],
                    bc(PB[5][:, 0:1], G), Alu.mult),
                   rd=["tc2", "pb5"], wr=["tc2"])
                op("vector", lambda e, rl=rl: e.tensor_tensor(
                    COLS["tc1"][:, :], rl, COLS["ylst"][:, :], Alu.mult),
                   rd=["aR", "ylst"], wr=["tc1"])
                op("vector", lambda e: e.tensor_add(
                    COLS["x1l8"][:, :], COLS["tc2"][:, :],
                    COLS["tc1"][:, :]), rd=["tc2", "tc1"], wr=["x1l8"])

        xload1(0)
        V1(0)
        V1b(0)
        for j in range(NCH):
            T1(j)
            if j + 1 < NCH:
                V1(j + 1)
                V1b(j + 1)
            W1ph(j)

        pN = NCH % 2
        op("vector", lambda e: e.tensor_copy(ccs[:, 0:8], a_st[pN][:, :]),
           rd=[f"a_st{pN}"], wr=["ccs"])
        op("vector", lambda e: e.tensor_copy(ccs[:, 8:16], p_st[pN][:, :]),
           rd=[f"p_st{pN}"], wr=["ccs"])
        op("vector", lambda e: e.tensor_copy(ccs[:, 16:24],
                                             COLS["x1l8"][:, :]),
           rd=["x1l8"], wr=["ccs"])
        s_ccin, n_ccin = dma("gpsimd", dram["ccin"][:, :], ccs[:, :],
                             rd=["ccs"])
        _wait("gpsimd", s_ccin, n_ccin)
        sems["sCC"]["n"] += 1
        nCC = sems["sCC"]["n"]
        hCC = sems["sCC"]["h"]
        EMIT["gpsimd"].append(lambda e: e.collective_compute(
            "AllGather", Alu.bypass,
            replica_groups=[[0, 1], [2, 3], [4, 5], [6, 7]],
            ins=[dram["ccin"][:, :]], outs=[dram["ccout"][:, :, :]],
        ).then_inc(hCC, 1))
        EMIT["gpsimd"].append(lambda e, v=nCC: e.wait_ge(hCC, v))
        dma("gpsimd", ccs[:, :], dram["ccout"][0], wr=["ccs"])
        op("vector", lambda e: e.tensor_tensor(
            COLS["a0c"][:, :], ccs[:, 0:8], bc(mask1[:, :], G), Alu.mult),
           rd=["ccs", "mask1"], wr=["a0c"])
        op("vector", lambda e: e.tensor_tensor(
            COLS["p0c"][:, :], ccs[:, 8:16], bc(mask1[:, :], G), Alu.mult),
           rd=["ccs", "mask1"], wr=["p0c"])
        op("vector", lambda e: e.tensor_copy(a_st[0][:, :],
                                             COLS["a0c"][:, :]),
           rd=["a0c"], wr=["a_st0"])
        op("vector", lambda e: e.tensor_copy(p_st[0][:, :],
                                             COLS["p0c"][:, :]),
           rd=["p0c"], wr=["p_st0"])
        op("vector", lambda e: e.tensor_copy(b_st[0][:, :],
                                             bc(mask1[:, :], G)),
           rd=["mask1"], wr=["b_st0"])
        op("vector", lambda e: e.tensor_tensor(
            COLS["x1l8"][:, :], ccs[:, 16:24], bc(mask1[:, :], G), Alu.mult),
           rd=["ccs", "mask1"], wr=["x1l8"])
        op("scalar", lambda e: e.square(COLS["tc2"][:, :],
                                        COLS["x1l8"][:, :]),
           rd=["x1l8"], wr=["tc2"])
        op("tensor", lambda e: e.matmul(
            PB[6][0:1, 0:G], onec[:, :], COLS["x1l8"][:, :], start=True,
            stop=True), rd=["x1l8", "onec"], wr=["pb6"])
        op("tensor", lambda e: e.matmul(
            PB[7][0:1, 0:G], onec[:, :], COLS["tc2"][:, :], start=True,
            stop=True), rd=["tc2", "onec"], wr=["pb7"])
        op("vector", lambda e: e.tensor_reduce(
            rowap("row_s", 1), PB[6][0:1, 0:G], AxX, Alu.add),
           rd=["pb6"], wr=[rown("row_s")])
        op("vector", lambda e: e.tensor_reduce(
            rowap("row_q", 1), PB[7][0:1, 0:G], AxX, Alu.add),
           rd=["pb7"], wr=[rown("row_q")])
        row_lnstats(simple=True, n=1)
        bcast_scalar("row_mu", 0, 4)
        bcast_scalar("row_r0", 0, 5)
        op("vector", lambda e: e.tensor_sub(
            COLS["tc2"][:, :], COLS["x1l8"][:, :], bc(PB[4][:, 0:1], G)),
           rd=["x1l8", "pb4"], wr=["tc2"])
        op("vector", lambda e: e.tensor_tensor(
            COLS["tc2"][:, :], COLS["tc2"][:, :], bc(PB[5][:, 0:1], G),
            Alu.mult), rd=["tc2", "pb5"], wr=["tc2"])
        op("vector", lambda e: e.tensor_tensor(
            COLS["tc2"][:, :], COLS["tc2"][:, :], vc8("g2"), Alu.mult),
           rd=["tc2", "vcs"], wr=["tc2"])
        op("vector", lambda e: e.tensor_add(
            COLS["tc2"][:, :], COLS["tc2"][:, :], vc8("b2")),
           rd=["tc2", "vcs"], wr=["tc2"])
        op("vector", lambda e: e.tensor_tensor(
            COLS["bnd_xb"][:, :], COLS["tc2"][:, :], bc(mask1[:, :], G),
            Alu.mult), rd=["tc2", "mask1"], wr=["bnd_xb"])

        barrier()

        dma("sync", c3(XOF(0)), dram["xT"][:, :, 0:TC], wr=[arn(XOF(0))])
        for j in range(NCH):
            Xj = XOF(j)
            if j + 1 < NCH:
                dma("sync", c3(XOF(j + 1)),
                    dram["xT"][:, :, (j + 1) * TC:(j + 2) * TC],
                    wr=[arn(XOF(j + 1))])
            dma("gpsimd", c3(E), dram["sp_ekw"][:, :, j * TC:(j + 1) * TC],
                wr=["aE"])
            dma("gpsimd", c3(Z), dram["sp_zv"][:, :, j * TC:(j + 1) * TC],
                wr=["aZ"])
            dma("gpsimd", c3(R), dram["sp_sgr"][:, :, j * TC:(j + 1) * TC],
                wr=["aR"])
            dma("sync", rowap("row_mu"),
                dram["sp_rows"][0:1, j * TC:(j + 1) * TC],
                wr=[rown("row_mu")])
            dma("sync", rowap("row_r0"),
                dram["sp_rows"][1:2, j * TC:(j + 1) * TC],
                wr=[rown("row_r0")])
            mk_murho("row_r0")
            bcasts("row_r0", "row_mr")
            op("vector", lambda e: e.tensor_copy(cg(P2, 0), PB[4][:, :]),
               rd=["pb4"], wr=["aP2"])
            op("vector", lambda e: e.tensor_copy(cg(P2, 1), PB[5][:, :]),
               rd=["pb5"], wr=["aP2"])
            for g in range(G):
                op("gpsimd", lambda e, g=g, s=cg(Xj, g): e.tensor_tensor(
                    cg(XB, g), s, cg(P2, 0), Alu.mult),
                   rd=[arn(Xj), "aP2"], wr=["aXB"])
                op("gpsimd", lambda e, g=g: e.tensor_sub(
                    cg(XB, g), cg(XB, g), cg(P2, 1)),
                   rd=["aXB", "aP2"], wr=["aXB"])
            pj2, pn2 = j % 2, (j + 1) % 2
            op("vector", lambda e, j=j, p=p_st[pj2]: e.tensor_sub(
                COLS["tc3"][:, :], p_sv[:, j * G:(j + 1) * G], p[:, :]),
               rd=["p_sv", f"p_st{pj2}"], wr=["tc3"])
            op("scalar", lambda e: e.activation(
                COLS["tc3"][:, :], COLS["tc3"][:, :], Act.Exp),
               rd=["tc3"], wr=["tc3"])
            for dg in range(G):
                op("vector", lambda e, dg=dg: e.tensor_scalar(
                    cg(E, dg), cg(E, dg), col(COLS["tc3"], dg), None,
                    Alu.mult), rd=["aE", "tc3"], wr=["aE"])
                op("gpsimd", lambda e, dg=dg: e.tensor_scalar(
                    cg(Z, dg), cg(Z, dg), col(COLS["tc3"], dg), None,
                    Alu.mult), rd=["aZ", "tc3"], wr=["aZ"])
            emit_wkv_chunk(
                j,
                Abuf=lambda dg: (gb(A, dg, 0, TC), gb(A, dg, 1, W1),
                                 gb(A, dg, 0, 1), gb(A, dg, TC, W1)),
                Bbuf=lambda dg: (gb(B, dg, 0, TC), gb(B, dg, 1, W1),
                                 gb(B, dg, 0, 1), gb(B, dg, TC, W1)),
                a_init=lambda dg, s=a_st[pj2]: col(s, dg),
                a_init_nm=f"a_st{pj2}",
                b_init=lambda dg, s=b_st[pj2]: col(s, dg),
                b_init_nm=f"b_st{pj2}",
                a_out=lambda dg, s=a_st[pn2]: col(s, dg),
                a_out_nm=f"a_st{pn2}",
                p_old=lambda dg, s=p_st[pj2]: col(s, dg),
                p_old_nm=f"p_st{pj2}",
                p_new=lambda dg, s=p_st[pn2]: col(s, dg),
                p_new_nm=f"p_st{pn2}",
                b_out=b_st[pn2], b_out_nm=f"b_st{pn2}",
                per_dg_bufs=lambda dg: (cg(P1, 0), cg(P2, 0), "aP1", "aP2"),
                save_mini=False)
            for dg in range(G):
                op("gpsimd", lambda e, dg=dg: e.tensor_scalar(
                    cg(P1, dg), cg(Z, dg), col(COLS["euw8"], dg), None,
                    Alu.mult), rd=["aZ", "euw8"], wr=["aP1"])
                op("gpsimd", lambda e, dg=dg: e.tensor_add(
                    cg(P1, dg), cg(P1, dg), gb(A, dg, 0, TC)),
                   rd=["aP1", "aA"], wr=["aP1"])
                op("vector", lambda e, dg=dg: e.scalar_tensor_tensor(
                    cg(P2, dg), cg(E, dg), col(COLS["euw8"], dg),
                    gb(B, dg, 0, TC), Alu.mult, Alu.add),
                   rd=["aE", "euw8", "aB"], wr=["aP2"])
            Ac = A[:, 0:G * TC]
            Bc = B[:, 0:G * TC]
            op("scalar", lambda e: e.activation(Ac, c3(P2), Act.Ln),
               rd=["aP2"], wr=["aA"])
            op("scalar", lambda e: e.activation(Bc, Ac, Act.Exp,
                                                scale=-1.0),
               rd=["aA"], wr=["aB"])
            op("vector", lambda e: e.tensor_tensor(
                Ac, P1[:, 0:G * TC], Bc, Alu.mult),
               rd=["aP1", "aB"], wr=["aA"])
            op("vector", lambda e: e.tensor_tensor(
                Bc, Ac, R[:, 0:G * TC], Alu.mult),
               rd=["aA", "aR"], wr=["aB"])
            op("vector", lambda e: e.tensor_add(
                Y[:, 0:G * TC], XB[:, 0:G * TC], Bc),
               rd=["aXB", "aB"], wr=["aY"])

            ln_stats(Y)
            row_lnstats(simple=True)
            mk_murho("row_r0")
            bcasts("row_r0", "row_mr")
            op("vector", lambda e: e.tensor_copy(
                XB[:, 0:G * W1:W1], COLS["bnd_xb"][:, :]),
               rd=["bnd_xb"], wr=["aXB"])
            apply_ln(Y, lambda g: (gb(XB, g, 1, W1), "aXB"),
                     affine=("g2", "b2"))
            op("vector", lambda e: e.tensor_copy(
                COLS["bnd_xb"][:, :], XB[:, TC:G * W1:W1]),
               rd=["aXB"], wr=["bnd_xb"])
            op("vector", lambda e: e.tensor_sub(c3(P1), sb_body(XB),
                                                sb_shift(XB)),
               rd=["aXB"], wr=["aP1"])
            for g in range(G):
                op("vector", lambda e, g=g: e.scalar_tensor_tensor(
                    cg(E, g), cg(P1, g), col(COLS["crkm8"], g),
                    gb(XB, g, 1, W1), Alu.mult, Alu.add),
                   rd=["aP1", "crkm8", "aXB"], wr=["aE"])
                op("gpsimd", lambda e, g=g: e.tensor_scalar(
                    cg(Z, g), cg(P1, g), col(COLS["crrm8"], g), None,
                    Alu.mult), rd=["aP1", "crrm8"], wr=["aZ"])
                op("gpsimd", lambda e, g=g: e.tensor_add(
                    cg(Z, g), cg(Z, g), gb(XB, g, 1, W1)),
                   rd=["aZ", "aXB"], wr=["aZ"])
            for dg in range(G):
                slot = dg % 2
                sl_ap = ring_a[:, slot * 1024:(slot + 1) * 1024]
                dma("sync", sl_ap, dram["wcr"][dg], wr=[f"ring{slot}"])
                for g in range(G):
                    op("tensor", lambda e, dg=dg, g=g, sl=sl_ap: e.matmul(
                        PB[dg % 2][:, :], sl[:, g * 128:(g + 1) * 128],
                        cg(Z, g), start=(g == 0), stop=(g == 7)),
                       rd=[f"ring{slot}", "aZ"], wr=[f"pb{dg % 2}"])
                op("scalar", lambda e, dg=dg: e.activation(
                    cg(R, dg), PB[dg % 2][:, :], Act.Sigmoid),
                   rd=[f"pb{dg % 2}"], wr=["aR"])
            kcslots = [A, B, Z, P1]

            def kc_ap(hg):
                return cg(kcslots[hg // 8], hg % 8)

            def kc_nm(hg):
                return arn(kcslots[hg // 8])

            for hg in range(HGK):
                slot = hg % 2
                sl_ap = ring_a[:, slot * 1024:(slot + 1) * 1024]
                dma("sync", sl_ap, dram["wck"][hg], wr=[f"ring{slot}"])
                for g in range(G):
                    op("tensor", lambda e, hg=hg, g=g, sl=sl_ap: e.matmul(
                        PB[2 + hg % 2][:, :], sl[:, g * 128:(g + 1) * 128],
                        cg(E, g), start=(g == 0), stop=(g == 7)),
                       rd=[f"ring{slot}", "aE"], wr=[f"pb{2 + hg % 2}"])
                op("scalar", lambda e, hg=hg: e.activation(
                    kc_ap(hg), PB[2 + hg % 2][:, :], Act.Relu),
                   rd=[f"pb{2 + hg % 2}"], wr=[kc_nm(hg)])
                op("vector", lambda e, hg=hg: e.tensor_tensor(
                    kc_ap(hg), kc_ap(hg), kc_ap(hg), Alu.mult),
                   rd=[kc_nm(hg)], wr=[kc_nm(hg)])
            for half in range(2):
                for dgl in range(4):
                    dg = half * 4 + dgl
                    for oc in range(4):
                        slot = oc % 2
                        sl_ap = ring_a[:, slot * 1024:(slot + 1) * 1024]
                        dma("sync", sl_ap,
                            dram["wcv"][dg][:, oc * 1024:(oc + 1) * 1024],
                            wr=[f"ring{slot}"])
                        for hh in range(8):
                            hg = oc * 8 + hh
                            op("tensor",
                               lambda e, hg=hg, hh=hh, sl=sl_ap, dgl=dgl:
                               e.matmul(
                                   PB[4 + dgl][:, :],
                                   sl[:, hh * 128:(hh + 1) * 128],
                                   kc_ap(hg), start=(hg == 0),
                                   stop=(hg == HGK - 1)),
                               rd=[f"ring{slot}", kc_nm(hg)],
                               wr=[f"pb{4 + dgl}"])
                    op("vector", lambda e, dg=dg, dgl=dgl: e.tensor_tensor(
                        cg(P2, dg), cg(R, dg), PB[4 + dgl][:, :], Alu.mult),
                       rd=["aR", f"pb{4 + dgl}"], wr=["aP2"])
            op("vector", lambda e: e.tensor_add(c3(XB), c3(Y), c3(P2)),
               rd=["aY", "aP2"], wr=["aXB"])
            op("vector", lambda e, s=c3(Xj): e.tensor_add(
                c3(XB), c3(XB), s), rd=["aXB", arn(Xj)], wr=["aXB"])
            sig("vector")
            dma("gpsimd", dram["sp_x2"][:, :, j * TC:(j + 1) * TC], c3(XB),
                rd=["aXB"])

        barrier()

        Ebf = E[:, 0:G * W1].bitcast(BF16)
        Abf = A[:, 0:G * W1].bitcast(BF16)

        def ybuf(jj):
            return (Ebf, "aE") if jj % 2 == 0 else (Abf, "aA")

        def y1part(jj, s, g):
            yb = Ebf if jj % 2 == 0 else Abf
            return yb[:, s * G * TC + g * TC:s * G * TC + (g + 1) * TC]

        def hh_ap(hg):
            if hg < 16:
                return B[:, 0:G * W1].bitcast(BF16)[
                    :, hg * TC:(hg + 1) * TC]
            return R[:, 0:G * W1].bitcast(BF16)[
                :, (hg - 16) * TC:(hg - 15) * TC]

        def hh_nm(hg):
            return "aB" if hg < 16 else "aR"

        def hl_ap(hg):
            if hg < 16:
                return Z[:, 0:G * W1].bitcast(BF16)[
                    :, hg * TC:(hg + 1) * TC]
            return XB[:, 0:G * W1].bitcast(BF16)[
                :, (hg - 16) * TC:(hg - 15) * TC]

        def hl_nm(hg):
            return "aZ" if hg < 16 else "aXB"

        def V2B(jj, phase):
            Xjj = XOF(jj)
            if phase == 0:
                ln_stats(Xjj, sqar=Y)
            elif phase == 1:
                row_lnstats(simple=True)
                mk_murho("row_r0")
                bcasts("row_r0", "row_mr", 6, 7)
            else:
                apply_ln(Xjj, lambda g: (cg(Y, g), "aY"), affine=None,
                         bi=6, bj=7)
                yb, ybnm = ybuf(jj)
                op("vector", lambda e, yb=yb, ybnm=ybnm: e.tensor_copy(
                    yb[:, 0:G * TC], Y[:, 0:G * TC]), rd=["aY"], wr=[ybnm])
                op("vector", lambda e, yb=yb, ybnm=ybnm: e.tensor_sub(
                    yb[:, G * TC:2 * G * TC], Y[:, 0:G * TC],
                    yb[:, 0:G * TC]), rd=["aY", ybnm], wr=[ybnm])

        dma("sync", c3(XOF(0)), dram["sp_x2"][:, :, 0:TC],
            wr=[arn(XOF(0))])
        V2B(0, 0)
        V2B(0, 1)
        V2B(0, 2)
        for j in range(NCH):
            Xj = XOF(j)
            ynm = ybuf(j)[1]
            if j + 1 < NCH:
                dma("sync", c3(XOF(j + 1)),
                    dram["sp_x2"][:, :, (j + 1) * TC:(j + 2) * TC],
                    wr=[arn(XOF(j + 1))])

            for hg in range(HGF):
                slot = hg % 2
                sl_g = ring_b[:, slot * 1024:(slot + 1) * 1024]
                sl_u = ring_b[:, (2 + slot) * 1024:(3 + slot) * 1024]
                dma("sync", sl_g, dram["wgbf"][hg], wr=[f"rbg{slot}"])
                dma("sync", sl_u, dram["wubf"][hg], wr=[f"rbu{slot}"])
                pg = 0 if hg % 2 == 0 else 2
                pu = 1 if hg % 2 == 0 else 3
                for s in range(2):
                    for g in range(G):
                        op("tensor", lambda e, s=s, g=g, sl=sl_g, pg=pg,
                           j=j: e.matmul(
                            PB[pg][:, :], sl[:, g * 128:(g + 1) * 128],
                            y1part(j, s, g), start=(s == 0 and g == 0),
                            stop=(s == 1 and g == 7)),
                           rd=[f"rbg{slot}", ynm], wr=[f"pb{pg}"])
                op("scalar", lambda e, pg=pg: e.activation(
                    cg(P1, 0), PB[pg][:, :], Act.Silu, scale=1.0 / 127.0),
                   rd=[f"pb{pg}"], wr=["aP1"])
                for s in range(2):
                    for g in range(G):
                        op("tensor", lambda e, s=s, g=g, sl=sl_u, pu=pu,
                           j=j: e.matmul(
                            PB[pu][:, :], sl[:, g * 128:(g + 1) * 128],
                            y1part(j, s, g), start=(s == 0 and g == 0),
                            stop=(s == 1 and g == 7)),
                           rd=[f"rbu{slot}", ynm], wr=[f"pb{pu}"])
                op("vector", lambda e, pu=pu: e.scalar_tensor_tensor(
                    cg(P1, 1), PB[pu][:, :], 1.0 / 127.0, cg(P1, 0),
                    Alu.mult, Alu.mult), rd=[f"pb{pu}", "aP1"], wr=["aP1"])
                op("vector", lambda e, hg=hg: e.tensor_copy(
                    hh_ap(hg), cg(P1, 1)), rd=["aP1"], wr=[hh_nm(hg)])
                op("vector", lambda e, hg=hg: e.tensor_sub(
                    hl_ap(hg), cg(P1, 1), hh_ap(hg)),
                   rd=["aP1", hh_nm(hg)], wr=[hl_nm(hg)])
                if j + 1 < NCH and hg in (1, 3, 5):
                    V2B(j + 1, (hg - 1) // 2)
            for dg in range(G):
                for pc in range(3):
                    slot = pc % 2
                    sl_d = ring_b[:, slot * 1024:(slot + 1) * 1024]
                    dma("sync", sl_d,
                        dram["wdbf"][dg][:, pc * 1024:(pc + 1) * 1024],
                        wr=[f"rbg{slot}"])
                    for hh in range(8):
                        hg = pc * 8 + hh
                        for s in range(2):
                            hap = hh_ap(hg) if s == 0 else hl_ap(hg)
                            hnm = hh_nm(hg) if s == 0 else hl_nm(hg)
                            op("tensor",
                               lambda e, hh=hh, s=s, h=hap, dg=dg, hg=hg,
                               sl=sl_d: e.matmul(
                                   PB[4 + dg % 2][:, :],
                                   sl[:, hh * 128:(hh + 1) * 128], h,
                                   start=(hg == 0 and s == 0),
                                   stop=(hg == HGF - 1 and s == 1)),
                               rd=[f"rbg{slot}", hnm],
                               wr=[f"pb{4 + dg % 2}"])
                op("vector", lambda e, dg=dg, s=cg(Xj, dg):
                   e.scalar_tensor_tensor(
                    cg(P2, dg), PB[4 + dg % 2][:, :], 1.0 / 127.0,
                    s, Alu.mult, Alu.add),
                   rd=[f"pb{4 + dg % 2}", arn(Xj)], wr=["aP2"])
            sig("vector")
            dma("gpsimd", dram["outT"][:, :, j * TC:(j + 1) * TC], c3(P2),
                rd=["aP2"])

        for i in range(DMA_ROT):
            nfin = sems[f"sG{i}"]["n"]
            hG = sems[f"sG{i}"]["h"]
            if nfin:
                EMIT["gpsimd"].append(
                    lambda e, v=nfin, h=hG: e.wait_ge(h, v))

        with nc.Block() as block:
            @block.sync
            def _(eng):
                for f in EMIT["sync"]:
                    f(eng)

            @block.vector
            def _(eng):
                for f in EMIT["vector"]:
                    f(eng)

            @block.scalar
            def _(eng):
                for f in EMIT["scalar"]:
                    f(eng)

            @block.tensor
            def _(eng):
                for f in EMIT["tensor"]:
                    f(eng)

            @block.gpsimd
            def _(eng):
                for f in EMIT["gpsimd"]:
                    f(eng)

    return nc



def _tile_kvr(w):
    return np.ascontiguousarray(
        w.reshape(G, 128, G, 128).transpose(0, 3, 2, 1).reshape(G, 128, C))


def _tile_ck(w):
    return np.ascontiguousarray(
        w.reshape(HGK, 128, G, 128).transpose(0, 3, 2, 1).reshape(
            HGK, 128, C))


def _tile_cv(w):
    return np.ascontiguousarray(
        w.reshape(G, 128, HGK, 128).transpose(0, 3, 2, 1).reshape(
            G, 128, HCK))


def _tile_ffn_up(w):
    return np.ascontiguousarray(
        w.reshape(HGF, 128, G, 128).transpose(0, 3, 2, 1).reshape(
            HGF, 128, C))


def _tile_ffn_dn(w):
    return np.ascontiguousarray(
        w.reshape(G, 128, HGF, 128).transpose(0, 3, 2, 1).reshape(
            G, 128, HFF))


def _vcs_pack(vs):
    a = np.stack([np.asarray(v, np.float32).reshape(G, 128).T for v in vs],
                 axis=1)
    return np.ascontiguousarray(a)


def make_in_maps(inp, TCORE):
    wk = _tile_kvr(np.asarray(inp["L0_attn_Wk"], np.float32))
    wv = _tile_kvr(np.asarray(inp["L0_attn_Wv"], np.float32))
    wr = _tile_kvr(np.asarray(inp["L0_attn_Wr"], np.float32))
    wcr = _tile_kvr(np.asarray(inp["L0_attn_Wcr"], np.float32))
    wckt = _tile_ck(np.asarray(inp["L0_attn_Wck"], np.float32))
    wcvt = _tile_cv(np.asarray(inp["L0_attn_Wcv"], np.float32))
    wg = _tile_ffn_up(np.asarray(inp["L0_ffn_w_gate"], np.int32))
    wu = _tile_ffn_up(np.asarray(inp["L0_ffn_w_up"], np.int32))
    wd = _tile_ffn_dn(np.asarray(inp["L0_ffn_w_down"], np.int32))
    vcs = _vcs_pack([inp["L0_attn_" + n] for n in (
        "w", "u", "rk", "rv", "rr", "crk", "crr", "ln1_g", "ln1_b",
        "ln2_g", "ln2_b")])
    x = np.asarray(inp["L0_attn_x"], np.float32)
    Bb = x.shape[0]
    in_maps = []
    for core in range(2 * Bb):
        b, h = core // 2, core % 2
        t0 = h * TCORE
        xs = x[b, t0:t0 + TCORE, :]
        xTc = np.ascontiguousarray(
            xs.T.reshape(G, 128, TCORE).transpose(1, 0, 2))
        if h == 0:
            xbnd = np.zeros((128, G), np.float32)
            maskv = 0.0
        else:
            xbnd = np.ascontiguousarray(x[b, t0 - 1, :].reshape(G, 128).T)
            maskv = 1.0
        cfgv = np.zeros((128, 4), np.float32)
        cfgv[:, 0] = maskv
        in_maps.append({
            "xT": xTc, "xbnd": xbnd, "wk": wk, "wv": wv, "wr": wr,
            "wcr": wcr, "wck": wckt, "wcv": wcvt, "wg": wg, "wu": wu,
            "wd": wd, "vcs": vcs, "cfg": cfgv,
        })
    return in_maps


def kernel(**inputs):
    x = np.asarray(inputs["L0_attn_x"], np.float32)
    Bb, Tt, Cc = x.shape
    assert Cc == C
    TCORE = Tt // 2
    nc = build_nc(TCORE=TCORE, TC=512)
    in_maps = make_in_maps(inputs, TCORE)
    res = run_bass_kernel_spmd(nc, in_maps, core_ids=list(range(8)))
    out = np.empty((Bb, Tt, Cc), np.float32)
    for core in range(8):
        b, h = core // 2, core % 2
        t0 = h * TCORE
        o = np.asarray(res.results[core]["outT"])
        out[b, t0:t0 + TCORE, :] = o.transpose(1, 0, 2).reshape(C, TCORE).T
    return out


# version 14
# speedup vs baseline: 1.1082x; 1.1082x over previous
import contextlib
import numpy as np
import concourse.bass as bass
import concourse.mybir as mybir
from concourse.bass_utils import run_bass_kernel_spmd

F32 = mybir.dt.float32
BF16 = mybir.dt.bfloat16
I32 = mybir.dt.int32
Act = mybir.ActivationFunctionType
Alu = mybir.AluOpType
AxX = mybir.AxisListType.X

C = 1024
G = 8
HCK = 4096
HGK = 32
HFF = 3072
HGF = 24
EPS = 1e-5
NEGBIG = -1.0e30
VIDX = {"w": 0, "u": 1, "rk": 2, "rv": 3, "rr": 4, "crk": 5, "crr": 6,
        "g1": 7, "b1": 8, "g2": 9, "b2": 10}
ENG_SEM = {"sync": "sQ", "gpsimd": "sP", "vector": "sV", "scalar": "sS",
           "tensor": "sT"}


def build_nc(TCORE=4096, TC=512, dbg=()):
    NCH = TCORE // TC
    assert NCH * TC == TCORE
    W1 = TC + 1

    nc = bass.Bass("TRN2", target_bir_lowering=False, debug=False)

    dram = {}
    specs = [
        ("xT", [128, G, TCORE], F32, "ExternalInput"),
        ("xbnd", [128, G], F32, "ExternalInput"),
        ("wk", [G, 128, C], F32, "ExternalInput"),
        ("wv", [G, 128, C], F32, "ExternalInput"),
        ("wr", [G, 128, C], F32, "ExternalInput"),
        ("wcr", [G, 128, C], F32, "ExternalInput"),
        ("wck", [HGK, 128, C], F32, "ExternalInput"),
        ("wcv", [G, 128, HCK], F32, "ExternalInput"),
        ("wg", [HGF, 128, C], I32, "ExternalInput"),
        ("wu", [HGF, 128, C], I32, "ExternalInput"),
        ("wd", [G, 128, HFF], I32, "ExternalInput"),
        ("vcs", [128, 11, G], F32, "ExternalInput"),
        ("cfg", [128, 4], F32, "ExternalInput"),
        ("outT", [128, G, TCORE], F32, "ExternalOutput"),
        ("sp_ekw", [128, G, TCORE], F32, None),
        ("sp_zv", [128, G, TCORE], F32, None),
        ("sp_sgr", [128, G, TCORE], F32, None),
        ("sp_x2", [128, G, TCORE], F32, None),
        ("wgbf", [HGF, 128, C], BF16, None),
        ("wubf", [HGF, 128, C], BF16, None),
        ("wdbf", [G, 128, HFF], BF16, None),
        ("sp_rows", [2, TCORE], F32, None),
        ("ccin", [128, 24], F32, None),
        ("ccout", [2, 128, 24], F32, None),
    ]
    for name, shape, dt, kind in specs:
        if kind is None:
            dram[name] = nc.dram_tensor(name, shape, dt).ap()
        else:
            dram[name] = nc.dram_tensor(name, shape, dt, kind=kind).ap()
    for name in dbg:
        dram["dbg_" + name] = nc.dram_tensor(
            "dbg_" + name, [128, G, TCORE], F32, kind="ExternalOutput").ap()

    est = contextlib.ExitStack()

    def sbuf(name, shape, dt=F32):
        return est.enter_context(nc.sbuf_tensor(name, shape, dt))

    with est:
        ARN = ["aE", "aZ", "aR", "aA", "aB", "aP1", "aP2", "aX", "aY", "aXB"]
        AR = {n: sbuf(n, [128, G * W1]) for n in ARN}
        X2ALT = sbuf("x2alt", [128, G * W1])
        ring_a = sbuf("ring_a", [128, 2 * 1024])
        ring_b = sbuf("ring_b", [128, 4 * 1024], BF16)
        vcs_sb = sbuf("vcs_sb", [128, 11 * G])
        cfg_sb = sbuf("cfg_sb", [128, 4])
        COLS = {n: sbuf(n, [128, G]) for n in (
            "ew8", "emw8", "euw8", "rkm8", "rvm8", "rrm8", "crkm8", "crrm8",
            "bias8",
            "tc1", "tc2", "tc3", "tc4", "ca8", "cb8", "f18", "a0c", "p0c",
            "x1l8", "bnd_xa", "bnd_xb", "ylst", "am8", "bm8")}
        a_st = [sbuf(f"a_st{i}", [128, G]) for i in range(2)]
        b_st = [sbuf(f"b_st{i}", [128, G]) for i in range(2)]
        p_st = [sbuf(f"p_st{i}", [128, G]) for i in range(2)]
        a_sv = sbuf("a_sv", [128, NCH * G])
        b_sv = sbuf("b_sv", [128, NCH * G])
        p_sv = sbuf("p_sv", [128, NCH * G])
        negb = sbuf("negb", [128, 1])
        eps1 = sbuf("eps1", [1, 1])
        onec = sbuf("onec", [128, 1])
        mask1 = sbuf("mask1", [128, 1])
        ccs = sbuf("ccs", [128, 24])
        ROWB = sbuf("rowb", [1, 5 * TC])
        RSLOT = {"row_s": 0, "row_mu": 0, "row_q": 1, "row_var": 1,
                 "row_t": 2, "row_mr": 2, "row_r0": 3, "row_rho": 4}

        def rowap(name, n=TC):
            s = RSLOT[name]
            return ROWB[0:1, s * TC:s * TC + n]

        def rown(name):
            return f"rows{RSLOT[name]}"

        PB = [est.enter_context(nc.psum_tensor(f"pb{i}", [128, TC], F32))
              for i in range(8)]

        EMIT = {k: [] for k in ("sync", "vector", "scalar", "tensor",
                                "gpsimd")}
        sems = {}
        DMA_ROT = 4
        sem_names = ["sV", "sS", "sT", "sP", "sCC"]
        sem_names += [f"sQ{i}" for i in range(DMA_ROT)]
        sem_names += [f"sG{i}" for i in range(DMA_ROT)]
        for s in sem_names:
            sems[s] = {"h": est.enter_context(nc.semaphore(s)), "n": 0}
        dma_idx = {"sync": 0, "gpsimd": 0}
        DMA_BASE = {"sync": "sQ", "gpsimd": "sG"}
        last_wait = {e: {} for e in EMIT}
        trk_w = {}
        trk_r = {}
        pend_w = {e: set() for e in EMIT}
        pend_r = {e: set() for e in EMIT}
        dirty = {e: set() for e in EMIT}
        dirty_r = {e: set() for e in EMIT}

        def _wait(eng, sem, val):
            if val is None or val <= 0:
                return
            if last_wait[eng].get(sem, 0) >= val:
                return
            last_wait[eng][sem] = val
            h = sems[sem]["h"]
            EMIT[eng].append(lambda e, h=h, v=val: e.wait_ge(h, v))

        def sig(eng):
            s = ENG_SEM[eng]
            if eng == "sync":
                pend_w[eng].clear()
                pend_r[eng].clear()
                dirty[eng].clear()
                return 0
            if not pend_w[eng] and not pend_r[eng]:
                return sems[s]["n"]
            sems[s]["n"] += 1
            n = sems[s]["n"]
            h = sems[s]["h"]
            EMIT[eng].append(lambda e, h=h: e.drain().then_inc(h, 1))
            for b in pend_w[eng]:
                trk_w[b] = (s, n)
            for b in pend_r[eng]:
                trk_r.setdefault(b, {})[s] = n
            pend_w[eng].clear()
            pend_r[eng].clear()
            dirty[eng].clear()
            dirty_r[eng].clear()
            return n

        def use(eng, bufs_r, bufs_w, dma_issue=False):
            own = ENG_SEM[eng]

            def is_own(sname):
                if sname == own:
                    return True
                if dma_issue and eng in DMA_BASE:
                    return sname.startswith(DMA_BASE[eng])
                return False
            need_drain = False
            for b in bufs_r:
                if b in dirty[eng]:
                    need_drain = True
                else:
                    for e2 in EMIT:
                        if e2 != eng and b in pend_w[e2]:
                            sig(e2)
            for b in bufs_w:
                if b in dirty_r[eng] or b in dirty[eng]:
                    need_drain = True
                for e2 in EMIT:
                    if e2 != eng and (b in pend_w[e2] or b in pend_r[e2]):
                        sig(e2)
            if need_drain:
                EMIT[eng].append(lambda e: e.drain())
                dirty[eng].clear()
                dirty_r[eng].clear()
            for b in bufs_r:
                w = trk_w.get(b)
                if w and not is_own(w[0]):
                    _wait(eng, w[0], w[1])
            for b in bufs_w:
                w = trk_w.get(b)
                if w and not is_own(w[0]):
                    _wait(eng, w[0], w[1])
                for s2, n2 in list(trk_r.get(b, {}).items()):
                    if not is_own(s2):
                        _wait(eng, s2, n2)

        def op(eng, fn, rd=(), wr=()):
            use(eng, bufs_r=rd, bufs_w=wr)
            EMIT[eng].append(fn)
            for b in wr:
                pend_w[eng].add(b)
                dirty[eng].add(b)
                trk_r.pop(b, None)
            for b in rd:
                pend_r[eng].add(b)
                dirty_r[eng].add(b)

        def dma(q, out_ap, in_ap, rd=(), wr=()):
            use(q, bufs_r=rd, bufs_w=wr, dma_issue=True)
            s = DMA_BASE[q] + str(dma_idx[q] % DMA_ROT)
            dma_idx[q] += 1
            _wait(q, s, sems[s]["n"])
            sems[s]["n"] += 16
            n = sems[s]["n"]
            h = sems[s]["h"]
            EMIT[q].append(lambda e, o=out_ap, i=in_ap, h=h:
                           e.dma_start(out=o, in_=i).then_inc(h, 16))
            for b in wr:
                trk_w[b] = (s, n)
                trk_r.pop(b, None)
            for b in rd:
                trk_r.setdefault(b, {})[s] = n
            return (s, n)

        def barrier():
            for e in EMIT:
                sig(e)
            for e in EMIT:
                for s in sems:
                    if s != "sCC":
                        _wait(e, s, sems[s]["n"])

        def c3(ar):
            return ar[:, 0:G * TC].rearrange("p (g t) -> p g t", g=G)

        def cg(ar, g):
            return ar[:, g * TC:(g + 1) * TC]

        def sb_body(ar):
            return ar[:, :].rearrange("p (g t) -> p g t", g=G)[:, :, 1:W1]

        def sb_shift(ar):
            return ar[:, :].rearrange("p (g t) -> p g t", g=G)[:, :, 0:TC]

        def gb(ar, g, lo, hi):
            return ar[:, g * W1 + lo:g * W1 + hi]

        def vc8(name):
            i = VIDX[name]
            return vcs_sb[:, i * G:(i + 1) * G]

        def col(t, i):
            return t[:, i:i + 1]

        def bc(colap, n=TC):
            return colap.to_broadcast((128, n))

        ARN_BY_ID = {}
        E, Z, R = AR["aE"], AR["aZ"], AR["aR"]
        A, B = AR["aA"], AR["aB"]
        P1, P2 = AR["aP1"], AR["aP2"]
        X, Y, XB = AR["aX"], AR["aY"], AR["aXB"]
        for k, v in AR.items():
            ARN_BY_ID[id(v)] = k
        ARN_BY_ID[id(X2ALT)] = "x2alt"

        def XOF(j):
            return X if j % 2 == 0 else X2ALT

        def arn(ar):
            return ARN_BY_ID[id(ar)]

        dma("sync", vcs_sb[:, :], dram["vcs"].rearrange("p v g -> p (v g)"),
            wr=["vcs"])
        dma("sync", cfg_sb[:, :], dram["cfg"][:, :], wr=["cfg"])
        dma("sync", E[:, 0:G], dram["xbnd"][:, :], wr=["aE"])

        op("scalar", lambda e: e.activation(COLS["ew8"][:, :], vc8("w"),
                                            Act.Exp), rd=["vcs"], wr=["ew8"])
        op("scalar", lambda e: e.activation(COLS["emw8"][:, :], vc8("w"),
                                            Act.Exp, scale=-1.0),
           rd=["vcs"], wr=["emw8"])
        op("vector", lambda e: e.tensor_sub(COLS["tc1"][:, :], vc8("u"),
                                            vc8("w")), rd=["vcs"], wr=["tc1"])
        op("scalar", lambda e: e.activation(COLS["euw8"][:, :],
                                            COLS["tc1"][:, :], Act.Exp),
           rd=["tc1"], wr=["euw8"])
        op("vector", lambda e: e.memset(negb[:, :], NEGBIG), wr=["negb"])
        op("vector", lambda e: e.memset(eps1[:, :], EPS), wr=["eps1"])
        op("vector", lambda e: e.memset(onec[:, :], 1.0), wr=["onec"])
        op("vector", lambda e: e.memset(a_st[0][:, :], 0.0), wr=["a_st0"])
        op("vector", lambda e: e.memset(b_st[0][:, :], 0.0), wr=["b_st0"])
        op("vector", lambda e: e.memset(p_st[0][:, :], 0.0), wr=["p_st0"])
        op("vector", lambda e: e.tensor_copy(mask1[:, :], cfg_sb[:, 0:1]),
           rd=["cfg"], wr=["mask1"])
        for nm, src in (("rkm8", "rk"), ("rvm8", "rv"), ("rrm8", "rr"),
                        ("crkm8", "crk"), ("crrm8", "crr")):
            op("vector", lambda e, nm=nm, src=src: e.tensor_scalar(
                COLS[nm][:, :], vc8(src), 1.0, -1.0, Alu.mult, Alu.add),
               rd=["vcs"], wr=[nm])

        def row_lnstats(simple, n=TC):
            op("vector", lambda e: e.tensor_scalar(
                rowap("row_mu", n), rowap("row_s", n), 1.0 / C, None,
                Alu.mult), rd=[rown("row_s")], wr=[rown("row_mu")])
            op("vector", lambda e: e.tensor_scalar(
                rowap("row_var", n), rowap("row_q", n), 1.0 / C, None,
                Alu.mult), rd=[rown("row_q")], wr=[rown("row_var")])
            op("vector", lambda e: e.tensor_tensor(
                rowap("row_t", n), rowap("row_mu", n), rowap("row_mu", n),
                Alu.mult), rd=[rown("row_mu")], wr=[rown("row_t")])
            op("vector", lambda e: e.tensor_sub(
                rowap("row_var", n), rowap("row_var", n), rowap("row_t", n)),
               rd=[rown("row_var"), rown("row_t")], wr=[rown("row_var")])
            op("scalar", lambda e: e.activation(
                rowap("row_t", n), rowap("row_var", n), Act.Ln,
                bias=eps1[0:1, 0:1]),
               rd=[rown("row_var"), "eps1"], wr=[rown("row_t")])
            op("scalar", lambda e: e.activation(
                rowap("row_r0", n), rowap("row_t", n), Act.Exp, scale=-0.5),
               rd=[rown("row_t")], wr=[rown("row_r0")])
            if simple:
                return
            op("scalar", lambda e: e.activation(
                rowap("row_rho", n), rowap("row_t", n), Act.Exp, scale=-1.0),
               rd=[rown("row_t")], wr=[rown("row_rho")])
            op("vector", lambda e: e.tensor_tensor(
                rowap("row_rho", n), rowap("row_var", n),
                rowap("row_rho", n), Alu.mult),
               rd=[rown("row_var"), rown("row_rho")], wr=[rown("row_rho")])
            op("scalar", lambda e: e.activation(
                rowap("row_t", n), rowap("row_rho", n), Act.Ln,
                bias=eps1[0:1, 0:1]),
               rd=[rown("row_rho"), "eps1"], wr=[rown("row_t")])
            op("scalar", lambda e: e.activation(
                rowap("row_rho", n), rowap("row_t", n), Act.Exp, scale=-0.5),
               rd=[rown("row_t")], wr=[rown("row_rho")])
            op("vector", lambda e: e.tensor_tensor(
                rowap("row_rho", n), rowap("row_r0", n), rowap("row_rho", n),
                Alu.mult),
               rd=[rown("row_r0"), rown("row_rho")], wr=[rown("row_rho")])

        def mk_murho(rho_name, n=TC):
            op("vector", lambda e: e.tensor_tensor(
                rowap("row_mr", n), rowap("row_mu", n), rowap(rho_name, n),
                Alu.mult),
               rd=[rown("row_mu"), rown(rho_name)], wr=[rown("row_mr")])

        def bcasts(r1, r2, bi=4, bj=5):
            op("tensor", lambda e, bi=bi: e.matmul(
                PB[bi][:, :], onec[0:1, 0:1].to_broadcast((1, 128)),
                rowap(r1), start=True, stop=True),
               rd=[rown(r1), "onec"], wr=[f"pb{bi}"])
            op("tensor", lambda e, bj=bj: e.matmul(
                PB[bj][:, :], onec[0:1, 0:1].to_broadcast((1, 128)),
                rowap(r2), start=True, stop=True),
               rd=[rown(r2), "onec"], wr=[f"pb{bj}"])

        def bcast_scalar(rname, off, pbi):
            op("tensor", lambda e: e.matmul(
                PB[pbi][:, 0:1], onec[0:1, 0:1].to_broadcast((1, 128)),
                ROWB[0:1, RSLOT[rname] * TC + off:RSLOT[rname] * TC + off + 1],
                start=True, stop=True),
               rd=[rown(rname), "onec"], wr=[f"pb{pbi}"])

        def ln_stats(src_ar, body=False, sqar=None):
            sq = A if sqar is None else sqar
            srcv = sb_body(src_ar) if body else c3(src_ar)
            op("scalar", lambda e: e.activation(c3(sq), srcv, Act.Square),
               rd=[arn(src_ar)], wr=[arn(sq)])
            for g in range(G):
                s_ap = gb(src_ar, g, 1, W1) if body else cg(src_ar, g)
                op("tensor", lambda e, s=s_ap, g=g: e.matmul(
                    PB[6][0:1, :], onec[:, :], s, start=(g == 0),
                    stop=(g == 7)), rd=[arn(src_ar), "onec"], wr=["pb6"])
            for g in range(G):
                op("tensor", lambda e, g=g: e.matmul(
                    PB[7][0:1, :], onec[:, :], cg(sq, g), start=(g == 0),
                    stop=(g == 7)), rd=[arn(sq), "onec"], wr=["pb7"])
            op("vector", lambda e: e.tensor_copy(rowap("row_s"),
                                                 PB[6][0:1, :]),
               rd=["pb6"], wr=[rown("row_s")])
            op("vector", lambda e: e.tensor_copy(rowap("row_q"),
                                                 PB[7][0:1, :]),
               rd=["pb7"], wr=[rown("row_q")])

        def apply_ln(src_ar, dst_ap_nm, affine, src_body=False,
                     bi=4, bj=5):
            for g in range(G):
                s_ap = gb(src_ar, g, 1, W1) if src_body else cg(src_ar, g)
                d_ap, d_nm = dst_ap_nm(g)
                op("vector", lambda e, s=s_ap, d=d_ap, bi=bi: e.tensor_tensor(
                    d, s, PB[bi][:, :], Alu.mult),
                   rd=[arn(src_ar), f"pb{bi}"], wr=[d_nm])
                op("vector", lambda e, d=d_ap, bj=bj: e.tensor_sub(
                    d, d, PB[bj][:, :]), rd=[d_nm, f"pb{bj}"], wr=[d_nm])
                if affine:
                    gn, bn = affine
                    op("vector", lambda e, d=d_ap, g=g: e.tensor_scalar(
                        d, d, col(vc8(gn), g), col(vc8(bn), g),
                        Alu.mult, Alu.add), rd=[d_nm, "vcs"], wr=[d_nm])

        op("scalar", lambda e: e.square(Z[:, 0:G], E[:, 0:G]),
           rd=["aE"], wr=["aZ"])
        op("tensor", lambda e: e.matmul(PB[6][0:1, 0:G], onec[:, :],
                                        E[:, 0:G], start=True, stop=True),
           rd=["onec", "aE"], wr=["pb6"])
        op("tensor", lambda e: e.matmul(PB[7][0:1, 0:G], onec[:, :],
                                        Z[:, 0:G], start=True, stop=True),
           rd=["onec", "aZ"], wr=["pb7"])
        op("vector", lambda e: e.tensor_reduce(
            rowap("row_s", 1), PB[6][0:1, 0:G], AxX, Alu.add),
           rd=["pb6"], wr=[rown("row_s")])
        op("vector", lambda e: e.tensor_reduce(
            rowap("row_q", 1), PB[7][0:1, 0:G], AxX, Alu.add),
           rd=["pb7"], wr=[rown("row_q")])
        row_lnstats(simple=False, n=1)
        bcast_scalar("row_mu", 0, 4)
        bcast_scalar("row_rho", 0, 5)
        op("vector", lambda e: e.tensor_sub(
            Z[:, 0:G], E[:, 0:G], bc(PB[4][:, 0:1], G)),
           rd=["aE", "pb4"], wr=["aZ"])
        op("vector", lambda e: e.tensor_tensor(
            Z[:, 0:G], Z[:, 0:G], bc(PB[5][:, 0:1], G), Alu.mult),
           rd=["aZ", "pb5"], wr=["aZ"])
        op("vector", lambda e: e.tensor_tensor(
            Z[:, 0:G], Z[:, 0:G], vc8("g1"), Alu.mult),
           rd=["aZ", "vcs"], wr=["aZ"])
        op("vector", lambda e: e.tensor_add(Z[:, 0:G], Z[:, 0:G], vc8("b1")),
           rd=["aZ", "vcs"], wr=["aZ"])
        op("vector", lambda e: e.tensor_tensor(
            COLS["bnd_xa"][:, :], Z[:, 0:G], bc(mask1[:, :], G), Alu.mult),
           rd=["aZ", "mask1"], wr=["bnd_xa"])

        par = 0
        for nm_i, nm_o, ngrp, wid in (("wg", "wgbf", HGF, C),
                                      ("wu", "wubf", HGF, C),
                                      ("wd", "wdbf", G, HFF)):
            for t in range(ngrp):
                for pp in range(wid // 1024):
                    src, sname = (X, "aX") if par == 0 else (P2, "aP2")
                    dst, dname = (Y, "aY") if par == 0 else (XB, "aXB")
                    par ^= 1
                    i32v = src[:, 0:1024].bitcast(I32)
                    bfv = dst[:, 0:512].bitcast(BF16)[:, 0:1024]
                    dma("sync", i32v,
                        dram[nm_i][t][:, pp * 1024:(pp + 1) * 1024],
                        wr=[sname])
                    op("vector", lambda e, o=bfv, i=i32v: e.tensor_copy(o, i),
                       rd=[sname], wr=[dname])
                    sig("vector")
                    dma("gpsimd",
                        dram[nm_o][t][:, pp * 1024:(pp + 1) * 1024], bfv,
                        rd=[dname])

        def emit_wkv_chunk(j, Abuf, Bbuf, a_init, a_init_nm, b_init,
                           b_init_nm, a_out, a_out_nm, p_old, p_old_nm,
                           p_new, p_new_nm, b_out, b_out_nm, per_dg_bufs,
                           save_mini):
            for dg in range(G):
                aS, aB, a0ap, aE_ = Abuf(dg)
                bS, bB, b0ap, bE_ = Bbuf(dg)
                mx_ap, gn_ap, mx_nm, gn_nm = per_dg_bufs(dg)
                ai = a_init(dg)
                bi = b_init(dg)
                op("vector", lambda e, b0ap=b0ap, bi=bi: e.tensor_copy(
                    b0ap, bi), rd=[b_init_nm], wr=["aB"])
                op("vector", lambda e, a0ap=a0ap, ai=ai: e.tensor_copy(
                    a0ap, ai), rd=[a_init_nm], wr=["aA"])
                op("vector", lambda e, dg=dg, bB=bB, bi=bi:
                   e.tensor_tensor_scan(
                    bB, bc(col(COLS["ew8"], dg)), cg(E, dg), bi,
                    Alu.mult, Alu.add),
                   rd=["ew8", "aE", b_init_nm], wr=["aB"])
                op("vector", lambda e, dg=dg, mx_ap=mx_ap, bS=bS:
                   e.scalar_tensor_tensor(
                    mx_ap, cg(E, dg), col(COLS["emw8"], dg), bS,
                    Alu.mult, Alu.max),
                   rd=["aE", "emw8", "aB"], wr=[mx_nm])
                op("scalar", lambda e, mx_ap=mx_ap, gn_ap=gn_ap:
                   e.activation(gn_ap, mx_ap, Act.Ln),
                   rd=[mx_nm], wr=[gn_nm])
                op("scalar", lambda e, gn_ap=gn_ap: e.activation(
                    gn_ap, gn_ap, Act.Exp, scale=-1.0),
                   rd=[gn_nm], wr=[gn_nm])
                op("vector", lambda e, mx_ap=mx_ap, gn_ap=gn_ap, bB=bB:
                   e.tensor_tensor(mx_ap, bB, gn_ap, Alu.mult),
                   rd=["aB", gn_nm], wr=[mx_nm])
                op("vector", lambda e, dg=dg, mx_ap=mx_ap, gn_ap=gn_ap:
                   e.scalar_tensor_tensor(
                    gn_ap, mx_ap, col(COLS["emw8"], dg), cg(Z, dg),
                    Alu.mult, Alu.mult),
                   rd=[mx_nm, "emw8", "aZ"], wr=[gn_nm])
                op("vector", lambda e, aB=aB, mx_ap=mx_ap, gn_ap=gn_ap,
                   ai=ai: e.tensor_tensor_scan(
                    aB, mx_ap, gn_ap, ai, Alu.mult, Alu.add),
                   rd=[gn_nm, mx_nm, a_init_nm], wr=["aA"])
                if save_mini:
                    op("vector", lambda e, dg=dg, aS=aS: e.tensor_copy(
                        col(COLS["am8"], dg), aS[:, TC - 1:TC]),
                       rd=["aA"], wr=["am8"])
                    op("vector", lambda e, dg=dg, bS=bS: e.tensor_copy(
                        col(COLS["bm8"], dg), bS[:, TC - 1:TC]),
                       rd=["aB"], wr=["bm8"])
                op("vector", lambda e, dg=dg, bE_=bE_: e.reciprocal(
                    col(COLS["tc1"], dg), bE_), rd=["aB"], wr=["tc1"])
                op("vector", lambda e, dg=dg, aE_=aE_, ao=a_out(dg):
                   e.tensor_tensor(ao, aE_, col(COLS["tc1"], dg), Alu.mult),
                   rd=["aA", "tc1"], wr=[a_out_nm])
                op("scalar", lambda e, dg=dg, bE_=bE_: e.activation(
                    col(COLS["tc2"], dg), bE_, Act.Ln),
                   rd=["aB"], wr=["tc2"])
                op("vector", lambda e, dg=dg, po=p_new(dg), pi=p_old(dg):
                   e.tensor_add(po, pi, col(COLS["tc2"], dg)),
                   rd=[p_old_nm, "tc2"], wr=[p_new_nm])
            op("vector", lambda e, t=b_out: e.memset(t[:, :], 1.0),
               wr=[b_out_nm])

        barrier()


        def xload1(j):
            dma("sync", c3(XOF(j)), dram["xT"][:, :, j * TC:(j + 1) * TC],
                wr=[arn(XOF(j))])

        def V1(j):
            Xj = XOF(j)
            if j + 1 < NCH:
                xload1(j + 1)
            ln_stats(Xj)
            row_lnstats(simple=False)
            dma("gpsimd", dram["sp_rows"][0:1, j * TC:(j + 1) * TC],
                rowap("row_mu"), rd=[rown("row_mu")])
            dma("gpsimd", dram["sp_rows"][1:2, j * TC:(j + 1) * TC],
                rowap("row_r0"), rd=[rown("row_r0")])
            mk_murho("row_rho")
            bcasts("row_rho", "row_mr")
            op("vector", lambda e: e.tensor_copy(
                XB[:, 0:G * W1:W1], COLS["bnd_xa"][:, :]),
               rd=["bnd_xa"], wr=["aXB"])
            apply_ln(Xj, lambda g: (gb(XB, g, 1, W1), "aXB"),
                     affine=("g1", "b1"))
            op("vector", lambda e: e.tensor_copy(
                COLS["bnd_xa"][:, :], XB[:, TC:G * W1:W1]),
               rd=["aXB"], wr=["bnd_xa"])
        def V1b(j):
            op("vector", lambda e: e.tensor_sub(c3(P2), sb_body(XB),
                                                sb_shift(XB)),
               rd=["aXB"], wr=["aP2"])
            for g in range(G):
                op("vector", lambda e, g=g: e.scalar_tensor_tensor(
                    cg(Y, g), cg(P2, g), col(COLS["rkm8"], g),
                    gb(XB, g, 1, W1), Alu.mult, Alu.add),
                   rd=["aP2", "rkm8", "aXB"], wr=["aY"])
            for g in range(G):
                op("vector", lambda e, g=g: e.scalar_tensor_tensor(
                    cg(P1, g), cg(P2, g), col(COLS["rvm8"], g),
                    gb(XB, g, 1, W1), Alu.mult, Alu.add),
                   rd=["aP2", "rvm8", "aXB"], wr=["aP1"])
            for g in range(G):
                op("vector", lambda e, g=g: e.scalar_tensor_tensor(
                    cg(P2, g), cg(P2, g), col(COLS["rrm8"], g),
                    gb(XB, g, 1, W1), Alu.mult, Alu.add),
                   rd=["aP2", "rrm8", "aXB"], wr=["aP2"])

        def T1(j):
            for mixrole, mnm, wnm, dst, dstn, act in (
                    ("P2", "aP2", "wr", R, "aR", Act.Sigmoid),
                    ("Y", "aY", "wk", E, "aE", None),
                    ("P1", "aP1", "wv", Z, "aZ", None)):
                Mt = {"Y": Y, "P1": P1, "P2": P2}[mixrole]
                for dg in range(G):
                    slot = dg % 2
                    sl_ap = ring_a[:, slot * 1024:(slot + 1) * 1024]
                    dma("sync", sl_ap, dram[wnm][dg], wr=[f"ring{slot}"])
                    pb_i = dg % 4
                    for g in range(G):
                        op("tensor", lambda e, dg=dg, g=g, sl=sl_ap,
                           ms=Mt, pb_i=pb_i, mnm=mnm: e.matmul(
                               PB[pb_i][:, :],
                               sl[:, g * 128:(g + 1) * 128], cg(ms, g),
                               start=(g == 0), stop=(g == 7)),
                           rd=[f"ring{slot}", mnm], wr=[f"pb{pb_i}"])
                    if act is None:
                        op("scalar", lambda e, dg=dg, d=dst, pb_i=pb_i:
                           e.copy(cg(d, dg), PB[pb_i][:, :]),
                           rd=[f"pb{pb_i}"], wr=[dstn])
                    else:
                        op("scalar", lambda e, dg=dg, d=dst, pb_i=pb_i,
                           act=act: e.activation(cg(d, dg), PB[pb_i][:, :],
                                                 act),
                           rd=[f"pb{pb_i}"], wr=[dstn])

        def W1ph(j):
            pj, pn = j % 2, (j + 1) % 2
            Xj = XOF(j)
            op("vector", lambda e, s=p_st[pj], j=j: e.tensor_copy(
                p_sv[:, j * G:(j + 1) * G], s[:, :]),
               rd=[f"p_st{pj}"], wr=["p_sv"])
            op("vector", lambda e, p=p_st[pj]: e.tensor_sub(
                COLS["bias8"][:, :], vc8("w"), p[:, :]),
               rd=["vcs", f"p_st{pj}"], wr=["bias8"])
            for dg in range(G):
                op("scalar", lambda e, dg=dg: e.activation(
                    cg(E, dg), cg(E, dg), Act.Exp,
                    bias=col(COLS["bias8"], dg)),
                   rd=["aE", "bias8"], wr=["aE"])
            for dg in range(G):
                op("vector", lambda e, dg=dg: e.tensor_tensor(
                    cg(Z, dg), cg(E, dg), cg(Z, dg), Alu.mult),
                   rd=["aE", "aZ"], wr=["aZ"])
            emit_wkv_chunk(
                j,
                Abuf=lambda dg: (A[:, 0:TC], A[:, 1:W1], A[:, 0:1],
                                 A[:, TC:W1]),
                Bbuf=lambda dg: (B[:, 0:TC], B[:, 1:W1], B[:, 0:1],
                                 B[:, TC:W1]),
                a_init=lambda dg, s=a_st[pj]: col(s, dg),
                a_init_nm=f"a_st{pj}",
                b_init=lambda dg, s=b_st[pj]: col(s, dg),
                b_init_nm=f"b_st{pj}",
                a_out=lambda dg, s=a_st[pn]: col(s, dg),
                a_out_nm=f"a_st{pn}",
                p_old=lambda dg, s=p_st[pj]: col(s, dg),
                p_old_nm=f"p_st{pj}",
                p_new=lambda dg, s=p_st[pn]: col(s, dg),
                p_new_nm=f"p_st{pn}",
                b_out=b_st[pn], b_out_nm=f"b_st{pn}",
                per_dg_bufs=lambda dg: (A[:, W1:W1 + TC],
                                        A[:, 2 * W1:2 * W1 + TC],
                                        "aA", "aA"),
                save_mini=(j == NCH - 1))

            sig("vector")
            sig("scalar")
            dma("gpsimd", dram["sp_ekw"][:, :, j * TC:(j + 1) * TC], c3(E),
                rd=["aE"])
            dma("gpsimd", dram["sp_zv"][:, :, j * TC:(j + 1) * TC], c3(Z),
                rd=["aZ"])
            dma("gpsimd", dram["sp_sgr"][:, :, j * TC:(j + 1) * TC], c3(R),
                rd=["aR"])

            if j == NCH - 1:
                zl = Z[:, TC - 1:G * TC:TC]
                el = E[:, TC - 1:G * TC:TC]
                xl = Xj[:, TC - 1:G * TC:TC]
                rl = R[:, TC - 1:G * TC:TC]
                op("vector", lambda e, zl=zl: e.tensor_tensor(
                    COLS["tc3"][:, :], zl, COLS["euw8"][:, :], Alu.mult),
                   rd=["aZ", "euw8"], wr=["tc3"])
                op("vector", lambda e: e.tensor_add(
                    COLS["tc3"][:, :], COLS["tc3"][:, :],
                    COLS["am8"][:, :]), rd=["tc3", "am8"], wr=["tc3"])
                op("vector", lambda e, el=el: e.tensor_tensor(
                    COLS["tc4"][:, :], el, COLS["euw8"][:, :], Alu.mult),
                   rd=["aE", "euw8"], wr=["tc4"])
                op("vector", lambda e: e.tensor_add(
                    COLS["tc4"][:, :], COLS["tc4"][:, :],
                    COLS["bm8"][:, :]), rd=["tc4", "bm8"], wr=["tc4"])
                op("vector", lambda e: e.reciprocal(
                    COLS["tc1"][:, 0:G], COLS["tc4"][:, :]),
                   rd=["tc4"], wr=["tc1"])
                op("vector", lambda e: e.tensor_tensor(
                    COLS["ylst"][:, :], COLS["tc3"][:, :],
                    COLS["tc1"][:, 0:G], Alu.mult),
                   rd=["tc3", "tc1"], wr=["ylst"])
                bcast_scalar("row_mu", TC - 1, 4)
                bcast_scalar("row_r0", TC - 1, 5)
                op("vector", lambda e, xl=xl: e.tensor_sub(
                    COLS["tc2"][:, :], xl, bc(PB[4][:, 0:1], G)),
                   rd=[arn(Xj), "pb4"], wr=["tc2"])
                op("vector", lambda e: e.tensor_tensor(
                    COLS["tc2"][:, :], COLS["tc2"][:, :],
                    bc(PB[5][:, 0:1], G), Alu.mult),
                   rd=["tc2", "pb5"], wr=["tc2"])
                op("vector", lambda e, rl=rl: e.tensor_tensor(
                    COLS["tc1"][:, :], rl, COLS["ylst"][:, :], Alu.mult),
                   rd=["aR", "ylst"], wr=["tc1"])
                op("vector", lambda e: e.tensor_add(
                    COLS["x1l8"][:, :], COLS["tc2"][:, :],
                    COLS["tc1"][:, :]), rd=["tc2", "tc1"], wr=["x1l8"])

        xload1(0)
        V1(0)
        V1b(0)
        for j in range(NCH):
            T1(j)
            if j + 1 < NCH:
                V1(j + 1)
                V1b(j + 1)
            W1ph(j)

        pN = NCH % 2
        op("vector", lambda e: e.tensor_copy(ccs[:, 0:8], a_st[pN][:, :]),
           rd=[f"a_st{pN}"], wr=["ccs"])
        op("vector", lambda e: e.tensor_copy(ccs[:, 8:16], p_st[pN][:, :]),
           rd=[f"p_st{pN}"], wr=["ccs"])
        op("vector", lambda e: e.tensor_copy(ccs[:, 16:24],
                                             COLS["x1l8"][:, :]),
           rd=["x1l8"], wr=["ccs"])
        s_ccin, n_ccin = dma("gpsimd", dram["ccin"][:, :], ccs[:, :],
                             rd=["ccs"])
        _wait("gpsimd", s_ccin, n_ccin)
        sems["sCC"]["n"] += 1
        nCC = sems["sCC"]["n"]
        hCC = sems["sCC"]["h"]
        EMIT["gpsimd"].append(lambda e: e.collective_compute(
            "AllGather", Alu.bypass,
            replica_groups=[[0, 1], [2, 3], [4, 5], [6, 7]],
            ins=[dram["ccin"][:, :]], outs=[dram["ccout"][:, :, :]],
        ).then_inc(hCC, 1))
        EMIT["gpsimd"].append(lambda e, v=nCC: e.wait_ge(hCC, v))
        dma("gpsimd", ccs[:, :], dram["ccout"][0], wr=["ccs"])
        op("vector", lambda e: e.tensor_tensor(
            COLS["a0c"][:, :], ccs[:, 0:8], bc(mask1[:, :], G), Alu.mult),
           rd=["ccs", "mask1"], wr=["a0c"])
        op("vector", lambda e: e.tensor_tensor(
            COLS["p0c"][:, :], ccs[:, 8:16], bc(mask1[:, :], G), Alu.mult),
           rd=["ccs", "mask1"], wr=["p0c"])
        op("vector", lambda e: e.tensor_copy(a_st[0][:, :],
                                             COLS["a0c"][:, :]),
           rd=["a0c"], wr=["a_st0"])
        op("vector", lambda e: e.tensor_copy(p_st[0][:, :],
                                             COLS["p0c"][:, :]),
           rd=["p0c"], wr=["p_st0"])
        op("vector", lambda e: e.tensor_copy(b_st[0][:, :],
                                             bc(mask1[:, :], G)),
           rd=["mask1"], wr=["b_st0"])
        op("vector", lambda e: e.tensor_tensor(
            COLS["x1l8"][:, :], ccs[:, 16:24], bc(mask1[:, :], G), Alu.mult),
           rd=["ccs", "mask1"], wr=["x1l8"])
        op("scalar", lambda e: e.square(COLS["tc2"][:, :],
                                        COLS["x1l8"][:, :]),
           rd=["x1l8"], wr=["tc2"])
        op("tensor", lambda e: e.matmul(
            PB[6][0:1, 0:G], onec[:, :], COLS["x1l8"][:, :], start=True,
            stop=True), rd=["x1l8", "onec"], wr=["pb6"])
        op("tensor", lambda e: e.matmul(
            PB[7][0:1, 0:G], onec[:, :], COLS["tc2"][:, :], start=True,
            stop=True), rd=["tc2", "onec"], wr=["pb7"])
        op("vector", lambda e: e.tensor_reduce(
            rowap("row_s", 1), PB[6][0:1, 0:G], AxX, Alu.add),
           rd=["pb6"], wr=[rown("row_s")])
        op("vector", lambda e: e.tensor_reduce(
            rowap("row_q", 1), PB[7][0:1, 0:G], AxX, Alu.add),
           rd=["pb7"], wr=[rown("row_q")])
        row_lnstats(simple=True, n=1)
        bcast_scalar("row_mu", 0, 4)
        bcast_scalar("row_r0", 0, 5)
        op("vector", lambda e: e.tensor_sub(
            COLS["tc2"][:, :], COLS["x1l8"][:, :], bc(PB[4][:, 0:1], G)),
           rd=["x1l8", "pb4"], wr=["tc2"])
        op("vector", lambda e: e.tensor_tensor(
            COLS["tc2"][:, :], COLS["tc2"][:, :], bc(PB[5][:, 0:1], G),
            Alu.mult), rd=["tc2", "pb5"], wr=["tc2"])
        op("vector", lambda e: e.tensor_tensor(
            COLS["tc2"][:, :], COLS["tc2"][:, :], vc8("g2"), Alu.mult),
           rd=["tc2", "vcs"], wr=["tc2"])
        op("vector", lambda e: e.tensor_add(
            COLS["tc2"][:, :], COLS["tc2"][:, :], vc8("b2")),
           rd=["tc2", "vcs"], wr=["tc2"])
        op("vector", lambda e: e.tensor_tensor(
            COLS["bnd_xb"][:, :], COLS["tc2"][:, :], bc(mask1[:, :], G),
            Alu.mult), rd=["tc2", "mask1"], wr=["bnd_xb"])

        barrier()

        dma("sync", c3(XOF(0)), dram["xT"][:, :, 0:TC], wr=[arn(XOF(0))])
        for j in range(NCH):
            Xj = XOF(j)
            if j + 1 < NCH:
                dma("sync", c3(XOF(j + 1)),
                    dram["xT"][:, :, (j + 1) * TC:(j + 2) * TC],
                    wr=[arn(XOF(j + 1))])
            dma("gpsimd", c3(E), dram["sp_ekw"][:, :, j * TC:(j + 1) * TC],
                wr=["aE"])
            dma("gpsimd", c3(Z), dram["sp_zv"][:, :, j * TC:(j + 1) * TC],
                wr=["aZ"])
            dma("gpsimd", c3(R), dram["sp_sgr"][:, :, j * TC:(j + 1) * TC],
                wr=["aR"])
            dma("sync", rowap("row_mu"),
                dram["sp_rows"][0:1, j * TC:(j + 1) * TC],
                wr=[rown("row_mu")])
            dma("sync", rowap("row_r0"),
                dram["sp_rows"][1:2, j * TC:(j + 1) * TC],
                wr=[rown("row_r0")])
            mk_murho("row_r0")
            bcasts("row_r0", "row_mr")
            for g in range(G):
                op("vector", lambda e, g=g, s=cg(Xj, g): e.tensor_tensor(
                    cg(XB, g), s, PB[4][:, :], Alu.mult),
                   rd=[arn(Xj), "pb4"], wr=["aXB"])
                op("vector", lambda e, g=g: e.tensor_sub(
                    cg(XB, g), cg(XB, g), PB[5][:, :]),
                   rd=["aXB", "pb5"], wr=["aXB"])
            pj2, pn2 = j % 2, (j + 1) % 2
            op("vector", lambda e, j=j, p=p_st[pj2]: e.tensor_sub(
                COLS["tc3"][:, :], p_sv[:, j * G:(j + 1) * G], p[:, :]),
               rd=["p_sv", f"p_st{pj2}"], wr=["tc3"])
            op("scalar", lambda e: e.activation(
                COLS["tc3"][:, :], COLS["tc3"][:, :], Act.Exp),
               rd=["tc3"], wr=["tc3"])
            for dg in range(G):
                op("vector", lambda e, dg=dg: e.tensor_scalar(
                    cg(E, dg), cg(E, dg), col(COLS["tc3"], dg), None,
                    Alu.mult), rd=["aE", "tc3"], wr=["aE"])
                op("vector", lambda e, dg=dg: e.tensor_scalar(
                    cg(Z, dg), cg(Z, dg), col(COLS["tc3"], dg), None,
                    Alu.mult), rd=["aZ", "tc3"], wr=["aZ"])
            emit_wkv_chunk(
                j,
                Abuf=lambda dg: (gb(A, dg, 0, TC), gb(A, dg, 1, W1),
                                 gb(A, dg, 0, 1), gb(A, dg, TC, W1)),
                Bbuf=lambda dg: (gb(B, dg, 0, TC), gb(B, dg, 1, W1),
                                 gb(B, dg, 0, 1), gb(B, dg, TC, W1)),
                a_init=lambda dg, s=a_st[pj2]: col(s, dg),
                a_init_nm=f"a_st{pj2}",
                b_init=lambda dg, s=b_st[pj2]: col(s, dg),
                b_init_nm=f"b_st{pj2}",
                a_out=lambda dg, s=a_st[pn2]: col(s, dg),
                a_out_nm=f"a_st{pn2}",
                p_old=lambda dg, s=p_st[pj2]: col(s, dg),
                p_old_nm=f"p_st{pj2}",
                p_new=lambda dg, s=p_st[pn2]: col(s, dg),
                p_new_nm=f"p_st{pn2}",
                b_out=b_st[pn2], b_out_nm=f"b_st{pn2}",
                per_dg_bufs=lambda dg: (cg(P1, 0), cg(P2, 0), "aP1", "aP2"),
                save_mini=False)
            for dg in range(G):
                op("vector", lambda e, dg=dg: e.scalar_tensor_tensor(
                    cg(P1, dg), cg(Z, dg), col(COLS["euw8"], dg),
                    gb(A, dg, 0, TC), Alu.mult, Alu.add),
                   rd=["aZ", "euw8", "aA"], wr=["aP1"])
                op("vector", lambda e, dg=dg: e.scalar_tensor_tensor(
                    cg(P2, dg), cg(E, dg), col(COLS["euw8"], dg),
                    gb(B, dg, 0, TC), Alu.mult, Alu.add),
                   rd=["aE", "euw8", "aB"], wr=["aP2"])
            Ac = A[:, 0:G * TC]
            Bc = B[:, 0:G * TC]
            op("scalar", lambda e: e.activation(Ac, c3(P2), Act.Ln),
               rd=["aP2"], wr=["aA"])
            op("scalar", lambda e: e.activation(Bc, Ac, Act.Exp,
                                                scale=-1.0),
               rd=["aA"], wr=["aB"])
            op("vector", lambda e: e.tensor_tensor(
                Ac, P1[:, 0:G * TC], Bc, Alu.mult),
               rd=["aP1", "aB"], wr=["aA"])
            op("vector", lambda e: e.tensor_tensor(
                Bc, Ac, R[:, 0:G * TC], Alu.mult),
               rd=["aA", "aR"], wr=["aB"])
            op("vector", lambda e: e.tensor_add(
                Y[:, 0:G * TC], XB[:, 0:G * TC], Bc),
               rd=["aXB", "aB"], wr=["aY"])

            ln_stats(Y)
            row_lnstats(simple=True)
            mk_murho("row_r0")
            bcasts("row_r0", "row_mr")
            op("vector", lambda e: e.tensor_copy(
                XB[:, 0:G * W1:W1], COLS["bnd_xb"][:, :]),
               rd=["bnd_xb"], wr=["aXB"])
            apply_ln(Y, lambda g: (gb(XB, g, 1, W1), "aXB"),
                     affine=("g2", "b2"))
            op("vector", lambda e: e.tensor_copy(
                COLS["bnd_xb"][:, :], XB[:, TC:G * W1:W1]),
               rd=["aXB"], wr=["bnd_xb"])
            op("vector", lambda e: e.tensor_sub(c3(P1), sb_body(XB),
                                                sb_shift(XB)),
               rd=["aXB"], wr=["aP1"])
            for g in range(G):
                op("vector", lambda e, g=g: e.scalar_tensor_tensor(
                    cg(E, g), cg(P1, g), col(COLS["crkm8"], g),
                    gb(XB, g, 1, W1), Alu.mult, Alu.add),
                   rd=["aP1", "crkm8", "aXB"], wr=["aE"])
                op("vector", lambda e, g=g: e.scalar_tensor_tensor(
                    cg(Z, g), cg(P1, g), col(COLS["crrm8"], g),
                    gb(XB, g, 1, W1), Alu.mult, Alu.add),
                   rd=["aP1", "crrm8", "aXB"], wr=["aZ"])
            for dg in range(G):
                slot = dg % 2
                sl_ap = ring_a[:, slot * 1024:(slot + 1) * 1024]
                dma("sync", sl_ap, dram["wcr"][dg], wr=[f"ring{slot}"])
                for g in range(G):
                    op("tensor", lambda e, dg=dg, g=g, sl=sl_ap: e.matmul(
                        PB[dg % 2][:, :], sl[:, g * 128:(g + 1) * 128],
                        cg(Z, g), start=(g == 0), stop=(g == 7)),
                       rd=[f"ring{slot}", "aZ"], wr=[f"pb{dg % 2}"])
                op("scalar", lambda e, dg=dg: e.activation(
                    cg(R, dg), PB[dg % 2][:, :], Act.Sigmoid),
                   rd=[f"pb{dg % 2}"], wr=["aR"])
            kcslots = [A, B, Z, P1]

            def kc_ap(hg):
                return cg(kcslots[hg // 8], hg % 8)

            def kc_nm(hg):
                return arn(kcslots[hg // 8])

            for hg in range(HGK):
                slot = hg % 2
                sl_ap = ring_a[:, slot * 1024:(slot + 1) * 1024]
                dma("sync", sl_ap, dram["wck"][hg], wr=[f"ring{slot}"])
                for g in range(G):
                    op("tensor", lambda e, hg=hg, g=g, sl=sl_ap: e.matmul(
                        PB[2 + hg % 2][:, :], sl[:, g * 128:(g + 1) * 128],
                        cg(E, g), start=(g == 0), stop=(g == 7)),
                       rd=[f"ring{slot}", "aE"], wr=[f"pb{2 + hg % 2}"])
                op("scalar", lambda e, hg=hg: e.activation(
                    kc_ap(hg), PB[2 + hg % 2][:, :], Act.Relu),
                   rd=[f"pb{2 + hg % 2}"], wr=[kc_nm(hg)])
                op("vector", lambda e, hg=hg: e.tensor_tensor(
                    kc_ap(hg), kc_ap(hg), kc_ap(hg), Alu.mult),
                   rd=[kc_nm(hg)], wr=[kc_nm(hg)])
            for half in range(2):
                for dgl in range(4):
                    dg = half * 4 + dgl
                    for oc in range(4):
                        slot = oc % 2
                        sl_ap = ring_a[:, slot * 1024:(slot + 1) * 1024]
                        dma("sync", sl_ap,
                            dram["wcv"][dg][:, oc * 1024:(oc + 1) * 1024],
                            wr=[f"ring{slot}"])
                        for hh in range(8):
                            hg = oc * 8 + hh
                            op("tensor",
                               lambda e, hg=hg, hh=hh, sl=sl_ap, dgl=dgl:
                               e.matmul(
                                   PB[4 + dgl][:, :],
                                   sl[:, hh * 128:(hh + 1) * 128],
                                   kc_ap(hg), start=(hg == 0),
                                   stop=(hg == HGK - 1)),
                               rd=[f"ring{slot}", kc_nm(hg)],
                               wr=[f"pb{4 + dgl}"])
                    op("vector", lambda e, dg=dg, dgl=dgl: e.tensor_tensor(
                        cg(P2, dg), cg(R, dg), PB[4 + dgl][:, :], Alu.mult),
                       rd=["aR", f"pb{4 + dgl}"], wr=["aP2"])
            op("vector", lambda e: e.tensor_add(c3(XB), c3(Y), c3(P2)),
               rd=["aY", "aP2"], wr=["aXB"])
            op("vector", lambda e, s=c3(Xj): e.tensor_add(
                c3(XB), c3(XB), s), rd=["aXB", arn(Xj)], wr=["aXB"])
            sig("vector")
            dma("gpsimd", dram["sp_x2"][:, :, j * TC:(j + 1) * TC], c3(XB),
                rd=["aXB"])

        barrier()

        Ebf = E[:, 0:G * W1].bitcast(BF16)
        Abf = A[:, 0:G * W1].bitcast(BF16)

        def ybuf(jj):
            return (Ebf, "aE") if jj % 2 == 0 else (Abf, "aA")

        def y1part(jj, s, g):
            yb = Ebf if jj % 2 == 0 else Abf
            return yb[:, s * G * TC + g * TC:s * G * TC + (g + 1) * TC]

        def hh_ap(hg):
            if hg < 16:
                return B[:, 0:G * W1].bitcast(BF16)[
                    :, hg * TC:(hg + 1) * TC]
            return R[:, 0:G * W1].bitcast(BF16)[
                :, (hg - 16) * TC:(hg - 15) * TC]

        def hh_nm(hg):
            return "aB" if hg < 16 else "aR"

        def hl_ap(hg):
            if hg < 16:
                return Z[:, 0:G * W1].bitcast(BF16)[
                    :, hg * TC:(hg + 1) * TC]
            return XB[:, 0:G * W1].bitcast(BF16)[
                :, (hg - 16) * TC:(hg - 15) * TC]

        def hl_nm(hg):
            return "aZ" if hg < 16 else "aXB"

        def V2B(jj, phase):
            Xjj = XOF(jj)
            if phase == 0:
                ln_stats(Xjj, sqar=Y)
            elif phase == 1:
                row_lnstats(simple=True)
                mk_murho("row_r0")
                bcasts("row_r0", "row_mr", 6, 7)
            else:
                apply_ln(Xjj, lambda g: (cg(Y, g), "aY"), affine=None,
                         bi=6, bj=7)
                yb, ybnm = ybuf(jj)
                op("vector", lambda e, yb=yb, ybnm=ybnm: e.tensor_copy(
                    yb[:, 0:G * TC], Y[:, 0:G * TC]), rd=["aY"], wr=[ybnm])
                op("vector", lambda e, yb=yb, ybnm=ybnm: e.tensor_sub(
                    yb[:, G * TC:2 * G * TC], Y[:, 0:G * TC],
                    yb[:, 0:G * TC]), rd=["aY", ybnm], wr=[ybnm])

        dma("sync", c3(XOF(0)), dram["sp_x2"][:, :, 0:TC],
            wr=[arn(XOF(0))])
        V2B(0, 0)
        V2B(0, 1)
        V2B(0, 2)
        for j in range(NCH):
            Xj = XOF(j)
            ynm = ybuf(j)[1]
            if j + 1 < NCH:
                dma("sync", c3(XOF(j + 1)),
                    dram["sp_x2"][:, :, (j + 1) * TC:(j + 2) * TC],
                    wr=[arn(XOF(j + 1))])

            for hg in range(HGF):
                slot = hg % 2
                sl_g = ring_b[:, slot * 1024:(slot + 1) * 1024]
                sl_u = ring_b[:, (2 + slot) * 1024:(3 + slot) * 1024]
                dma("sync", sl_g, dram["wgbf"][hg], wr=[f"rbg{slot}"])
                dma("sync", sl_u, dram["wubf"][hg], wr=[f"rbu{slot}"])
                pg = 0 if hg % 2 == 0 else 2
                pu = 1 if hg % 2 == 0 else 3
                for s in range(2):
                    for g in range(G):
                        op("tensor", lambda e, s=s, g=g, sl=sl_g, pg=pg,
                           j=j: e.matmul(
                            PB[pg][:, :], sl[:, g * 128:(g + 1) * 128],
                            y1part(j, s, g), start=(s == 0 and g == 0),
                            stop=(s == 1 and g == 7)),
                           rd=[f"rbg{slot}", ynm], wr=[f"pb{pg}"])
                op("scalar", lambda e, pg=pg: e.activation(
                    cg(P1, 0), PB[pg][:, :], Act.Silu, scale=1.0 / 127.0),
                   rd=[f"pb{pg}"], wr=["aP1"])
                for s in range(2):
                    for g in range(G):
                        op("tensor", lambda e, s=s, g=g, sl=sl_u, pu=pu,
                           j=j: e.matmul(
                            PB[pu][:, :], sl[:, g * 128:(g + 1) * 128],
                            y1part(j, s, g), start=(s == 0 and g == 0),
                            stop=(s == 1 and g == 7)),
                           rd=[f"rbu{slot}", ynm], wr=[f"pb{pu}"])
                op("vector", lambda e, pu=pu: e.scalar_tensor_tensor(
                    cg(P1, 1), PB[pu][:, :], 1.0 / 127.0, cg(P1, 0),
                    Alu.mult, Alu.mult), rd=[f"pb{pu}", "aP1"], wr=["aP1"])
                op("vector", lambda e, hg=hg: e.tensor_copy(
                    hh_ap(hg), cg(P1, 1)), rd=["aP1"], wr=[hh_nm(hg)])
                op("vector", lambda e, hg=hg: e.tensor_sub(
                    hl_ap(hg), cg(P1, 1), hh_ap(hg)),
                   rd=["aP1", hh_nm(hg)], wr=[hl_nm(hg)])
                if j + 1 < NCH and hg in (1, 3, 5):
                    V2B(j + 1, (hg - 1) // 2)
            for dg in range(G):
                for pc in range(3):
                    slot = pc % 2
                    sl_d = ring_b[:, slot * 1024:(slot + 1) * 1024]
                    dma("sync", sl_d,
                        dram["wdbf"][dg][:, pc * 1024:(pc + 1) * 1024],
                        wr=[f"rbg{slot}"])
                    for hh in range(8):
                        hg = pc * 8 + hh
                        for s in range(2):
                            hap = hh_ap(hg) if s == 0 else hl_ap(hg)
                            hnm = hh_nm(hg) if s == 0 else hl_nm(hg)
                            op("tensor",
                               lambda e, hh=hh, s=s, h=hap, dg=dg, hg=hg,
                               sl=sl_d: e.matmul(
                                   PB[4 + dg % 2][:, :],
                                   sl[:, hh * 128:(hh + 1) * 128], h,
                                   start=(hg == 0 and s == 0),
                                   stop=(hg == HGF - 1 and s == 1)),
                               rd=[f"rbg{slot}", hnm],
                               wr=[f"pb{4 + dg % 2}"])
                op("vector", lambda e, dg=dg, s=cg(Xj, dg):
                   e.scalar_tensor_tensor(
                    cg(P2, dg), PB[4 + dg % 2][:, :], 1.0 / 127.0,
                    s, Alu.mult, Alu.add),
                   rd=[f"pb{4 + dg % 2}", arn(Xj)], wr=["aP2"])
            sig("vector")
            dma("gpsimd", dram["outT"][:, :, j * TC:(j + 1) * TC], c3(P2),
                rd=["aP2"])

        for i in range(DMA_ROT):
            nfin = sems[f"sG{i}"]["n"]
            hG = sems[f"sG{i}"]["h"]
            if nfin:
                EMIT["gpsimd"].append(
                    lambda e, v=nfin, h=hG: e.wait_ge(h, v))

        with nc.Block() as block:
            @block.sync
            def _(eng):
                for f in EMIT["sync"]:
                    f(eng)

            @block.vector
            def _(eng):
                for f in EMIT["vector"]:
                    f(eng)

            @block.scalar
            def _(eng):
                for f in EMIT["scalar"]:
                    f(eng)

            @block.tensor
            def _(eng):
                for f in EMIT["tensor"]:
                    f(eng)

            @block.gpsimd
            def _(eng):
                for f in EMIT["gpsimd"]:
                    f(eng)

    return nc



def _tile_kvr(w):
    return np.ascontiguousarray(
        w.reshape(G, 128, G, 128).transpose(0, 3, 2, 1).reshape(G, 128, C))


def _tile_ck(w):
    return np.ascontiguousarray(
        w.reshape(HGK, 128, G, 128).transpose(0, 3, 2, 1).reshape(
            HGK, 128, C))


def _tile_cv(w):
    return np.ascontiguousarray(
        w.reshape(G, 128, HGK, 128).transpose(0, 3, 2, 1).reshape(
            G, 128, HCK))


def _tile_ffn_up(w):
    return np.ascontiguousarray(
        w.reshape(HGF, 128, G, 128).transpose(0, 3, 2, 1).reshape(
            HGF, 128, C))


def _tile_ffn_dn(w):
    return np.ascontiguousarray(
        w.reshape(G, 128, HGF, 128).transpose(0, 3, 2, 1).reshape(
            G, 128, HFF))


def _vcs_pack(vs):
    a = np.stack([np.asarray(v, np.float32).reshape(G, 128).T for v in vs],
                 axis=1)
    return np.ascontiguousarray(a)


def make_in_maps(inp, TCORE):
    wk = _tile_kvr(np.asarray(inp["L0_attn_Wk"], np.float32))
    wv = _tile_kvr(np.asarray(inp["L0_attn_Wv"], np.float32))
    wr = _tile_kvr(np.asarray(inp["L0_attn_Wr"], np.float32))
    wcr = _tile_kvr(np.asarray(inp["L0_attn_Wcr"], np.float32))
    wckt = _tile_ck(np.asarray(inp["L0_attn_Wck"], np.float32))
    wcvt = _tile_cv(np.asarray(inp["L0_attn_Wcv"], np.float32))
    wg = _tile_ffn_up(np.asarray(inp["L0_ffn_w_gate"], np.int32))
    wu = _tile_ffn_up(np.asarray(inp["L0_ffn_w_up"], np.int32))
    wd = _tile_ffn_dn(np.asarray(inp["L0_ffn_w_down"], np.int32))
    vcs = _vcs_pack([inp["L0_attn_" + n] for n in (
        "w", "u", "rk", "rv", "rr", "crk", "crr", "ln1_g", "ln1_b",
        "ln2_g", "ln2_b")])
    x = np.asarray(inp["L0_attn_x"], np.float32)
    Bb = x.shape[0]
    in_maps = []
    for core in range(2 * Bb):
        b, h = core // 2, core % 2
        t0 = h * TCORE
        xs = x[b, t0:t0 + TCORE, :]
        xTc = np.ascontiguousarray(
            xs.T.reshape(G, 128, TCORE).transpose(1, 0, 2))
        if h == 0:
            xbnd = np.zeros((128, G), np.float32)
            maskv = 0.0
        else:
            xbnd = np.ascontiguousarray(x[b, t0 - 1, :].reshape(G, 128).T)
            maskv = 1.0
        cfgv = np.zeros((128, 4), np.float32)
        cfgv[:, 0] = maskv
        in_maps.append({
            "xT": xTc, "xbnd": xbnd, "wk": wk, "wv": wv, "wr": wr,
            "wcr": wcr, "wck": wckt, "wcv": wcvt, "wg": wg, "wu": wu,
            "wd": wd, "vcs": vcs, "cfg": cfgv,
        })
    return in_maps


def kernel(**inputs):
    x = np.asarray(inputs["L0_attn_x"], np.float32)
    Bb, Tt, Cc = x.shape
    assert Cc == C
    TCORE = Tt // 2
    nc = build_nc(TCORE=TCORE, TC=512)
    in_maps = make_in_maps(inputs, TCORE)
    res = run_bass_kernel_spmd(nc, in_maps, core_ids=list(range(8)))
    out = np.empty((Bb, Tt, Cc), np.float32)
    for core in range(8):
        b, h = core // 2, core % 2
        t0 = h * TCORE
        o = np.asarray(res.results[core]["outT"])
        out[b, t0:t0 + TCORE, :] = o.transpose(1, 0, 2).reshape(C, TCORE).T
    return out


# version 16
# speedup vs baseline: 1.1125x; 1.0039x over previous
import contextlib
import numpy as np
import concourse.bass as bass
import concourse.mybir as mybir
from concourse.bass_utils import run_bass_kernel_spmd

F32 = mybir.dt.float32
BF16 = mybir.dt.bfloat16
I32 = mybir.dt.int32
Act = mybir.ActivationFunctionType
Alu = mybir.AluOpType
AxX = mybir.AxisListType.X

C = 1024
G = 8
HCK = 4096
HGK = 32
HFF = 3072
HGF = 24
EPS = 1e-5
NEGBIG = -1.0e30
VIDX = {"w": 0, "u": 1, "rk": 2, "rv": 3, "rr": 4, "crk": 5, "crr": 6,
        "g1": 7, "b1": 8, "g2": 9, "b2": 10}
ENG_SEM = {"sync": "sQ", "gpsimd": "sP", "vector": "sV", "scalar": "sS",
           "tensor": "sT"}


def build_nc(TCORE=4096, TC=512, dbg=()):
    NCH = TCORE // TC
    assert NCH * TC == TCORE
    W1 = TC + 1

    nc = bass.Bass("TRN2", target_bir_lowering=False, debug=False)

    dram = {}
    specs = [
        ("xT", [128, G, TCORE], F32, "ExternalInput"),
        ("xbnd", [128, G], F32, "ExternalInput"),
        ("wk", [G, 128, C], F32, "ExternalInput"),
        ("wv", [G, 128, C], F32, "ExternalInput"),
        ("wr", [G, 128, C], F32, "ExternalInput"),
        ("wcr", [G, 128, C], F32, "ExternalInput"),
        ("wck", [HGK, 128, C], F32, "ExternalInput"),
        ("wcv", [G, 128, HCK], F32, "ExternalInput"),
        ("wg", [HGF, 128, C], I32, "ExternalInput"),
        ("wu", [HGF, 128, C], I32, "ExternalInput"),
        ("wd", [G, 128, HFF], I32, "ExternalInput"),
        ("vcs", [128, 11, G], F32, "ExternalInput"),
        ("cfg", [128, 4], F32, "ExternalInput"),
        ("outT", [128, G, TCORE], F32, "ExternalOutput"),
        ("sp_ekw", [128, G, TCORE], F32, None),
        ("sp_zv", [128, G, TCORE], F32, None),
        ("sp_sgr", [128, G, TCORE], F32, None),
        ("sp_x2", [128, G, TCORE], F32, None),
        ("wgbf", [HGF, 128, C], BF16, None),
        ("wubf", [HGF, 128, C], BF16, None),
        ("wdbf", [G, 128, HFF], BF16, None),
        ("sp_rows", [2, TCORE], F32, None),
        ("ccin", [128, 24], F32, None),
        ("ccout", [2, 128, 24], F32, None),
    ]
    for name, shape, dt, kind in specs:
        if kind is None:
            dram[name] = nc.dram_tensor(name, shape, dt).ap()
        else:
            dram[name] = nc.dram_tensor(name, shape, dt, kind=kind).ap()
    for name in dbg:
        dram["dbg_" + name] = nc.dram_tensor(
            "dbg_" + name, [128, G, TCORE], F32, kind="ExternalOutput").ap()

    est = contextlib.ExitStack()

    def sbuf(name, shape, dt=F32):
        return est.enter_context(nc.sbuf_tensor(name, shape, dt))

    with est:
        ARN = ["aE", "aZ", "aR", "aA", "aB", "aP1", "aP2", "aX", "aY", "aXB"]
        AR = {n: sbuf(n, [128, G * W1]) for n in ARN}
        X2ALT = sbuf("x2alt", [128, G * W1])
        ring_a = sbuf("ring_a", [128, 2 * 1024])
        ring_b = sbuf("ring_b", [128, 4 * 1024], BF16)
        vcs_sb = sbuf("vcs_sb", [128, 11 * G])
        cfg_sb = sbuf("cfg_sb", [128, 4])
        COLS = {n: sbuf(n, [128, G]) for n in (
            "ew8", "emw8", "euw8", "rkm8", "rvm8", "rrm8", "crkm8", "crrm8",
            "bias8",
            "tc1", "tc2", "tc3", "tc4", "ca8", "cb8", "f18", "a0c", "p0c",
            "x1l8", "bnd_xa", "bnd_xb", "ylst", "am8", "bm8")}
        a_st = [sbuf(f"a_st{i}", [128, G]) for i in range(2)]
        b_st = [sbuf(f"b_st{i}", [128, G]) for i in range(2)]
        p_st = [sbuf(f"p_st{i}", [128, G]) for i in range(2)]
        a_sv = sbuf("a_sv", [128, NCH * G])
        b_sv = sbuf("b_sv", [128, NCH * G])
        p_sv = sbuf("p_sv", [128, NCH * G])
        negb = sbuf("negb", [128, 1])
        eps1 = sbuf("eps1", [1, 1])
        onec = sbuf("onec", [128, 1])
        mask1 = sbuf("mask1", [128, 1])
        ccs = sbuf("ccs", [128, 24])
        ROWB = sbuf("rowb", [1, 5 * TC])
        RSLOT = {"row_s": 0, "row_mu": 0, "row_q": 1, "row_var": 1,
                 "row_t": 2, "row_mr": 2, "row_r0": 3, "row_rho": 4}

        def rowap(name, n=TC):
            s = RSLOT[name]
            return ROWB[0:1, s * TC:s * TC + n]

        def rown(name):
            return f"rows{RSLOT[name]}"

        PB = [est.enter_context(nc.psum_tensor(f"pb{i}", [128, TC], F32))
              for i in range(8)]

        EMIT = {k: [] for k in ("sync", "vector", "scalar", "tensor",
                                "gpsimd")}
        sems = {}
        DMA_ROT = 4
        sem_names = ["sV", "sS", "sT", "sP", "sCC"]
        sem_names += [f"sQ{i}" for i in range(DMA_ROT)]
        sem_names += [f"sG{i}" for i in range(DMA_ROT)]
        for s in sem_names:
            sems[s] = {"h": est.enter_context(nc.semaphore(s)), "n": 0}
        dma_idx = {"sync": 0, "gpsimd": 0}
        DMA_BASE = {"sync": "sQ", "gpsimd": "sG"}
        last_wait = {e: {} for e in EMIT}
        trk_w = {}
        trk_r = {}
        pend_w = {e: set() for e in EMIT}
        pend_r = {e: set() for e in EMIT}
        dirty = {e: set() for e in EMIT}
        dirty_r = {e: set() for e in EMIT}

        def _wait(eng, sem, val):
            if val is None or val <= 0:
                return
            if last_wait[eng].get(sem, 0) >= val:
                return
            last_wait[eng][sem] = val
            h = sems[sem]["h"]
            EMIT[eng].append(lambda e, h=h, v=val: e.wait_ge(h, v))

        def sig(eng):
            s = ENG_SEM[eng]
            if eng == "sync":
                pend_w[eng].clear()
                pend_r[eng].clear()
                dirty[eng].clear()
                return 0
            if not pend_w[eng] and not pend_r[eng]:
                return sems[s]["n"]
            sems[s]["n"] += 1
            n = sems[s]["n"]
            h = sems[s]["h"]
            EMIT[eng].append(lambda e, h=h: e.drain().then_inc(h, 1))
            for b in pend_w[eng]:
                trk_w[b] = (s, n)
            for b in pend_r[eng]:
                trk_r.setdefault(b, {})[s] = n
            pend_w[eng].clear()
            pend_r[eng].clear()
            dirty[eng].clear()
            dirty_r[eng].clear()
            return n

        def use(eng, bufs_r, bufs_w, dma_issue=False):
            own = ENG_SEM[eng]

            def is_own(sname):
                if sname == own:
                    return True
                if dma_issue and eng in DMA_BASE:
                    return sname.startswith(DMA_BASE[eng])
                return False
            need_drain = False
            for b in bufs_r:
                if b in dirty[eng]:
                    need_drain = True
                else:
                    for e2 in EMIT:
                        if e2 != eng and b in pend_w[e2]:
                            sig(e2)
            for b in bufs_w:
                if b in dirty_r[eng] or b in dirty[eng]:
                    need_drain = True
                for e2 in EMIT:
                    if e2 != eng and (b in pend_w[e2] or b in pend_r[e2]):
                        sig(e2)
            if need_drain:
                EMIT[eng].append(lambda e: e.drain())
                dirty[eng].clear()
                dirty_r[eng].clear()
            for b in bufs_r:
                w = trk_w.get(b)
                if w and not is_own(w[0]):
                    _wait(eng, w[0], w[1])
            for b in bufs_w:
                w = trk_w.get(b)
                if w and not is_own(w[0]):
                    _wait(eng, w[0], w[1])
                for s2, n2 in list(trk_r.get(b, {}).items()):
                    if not is_own(s2):
                        _wait(eng, s2, n2)

        def op(eng, fn, rd=(), wr=()):
            use(eng, bufs_r=rd, bufs_w=wr)
            EMIT[eng].append(fn)
            for b in wr:
                pend_w[eng].add(b)
                dirty[eng].add(b)
                trk_r.pop(b, None)
            for b in rd:
                pend_r[eng].add(b)
                dirty_r[eng].add(b)

        def dma(q, out_ap, in_ap, rd=(), wr=()):
            use(q, bufs_r=rd, bufs_w=wr, dma_issue=True)
            s = DMA_BASE[q] + str(dma_idx[q] % DMA_ROT)
            dma_idx[q] += 1
            _wait(q, s, sems[s]["n"])
            sems[s]["n"] += 16
            n = sems[s]["n"]
            h = sems[s]["h"]
            EMIT[q].append(lambda e, o=out_ap, i=in_ap, h=h:
                           e.dma_start(out=o, in_=i).then_inc(h, 16))
            for b in wr:
                trk_w[b] = (s, n)
                trk_r.pop(b, None)
            for b in rd:
                trk_r.setdefault(b, {})[s] = n
            return (s, n)

        def barrier():
            for e in EMIT:
                sig(e)
            for e in EMIT:
                for s in sems:
                    if s != "sCC":
                        _wait(e, s, sems[s]["n"])

        def c3(ar):
            return ar[:, 0:G * TC].rearrange("p (g t) -> p g t", g=G)

        def cg(ar, g):
            return ar[:, g * TC:(g + 1) * TC]

        def sb_body(ar):
            return ar[:, :].rearrange("p (g t) -> p g t", g=G)[:, :, 1:W1]

        def sb_shift(ar):
            return ar[:, :].rearrange("p (g t) -> p g t", g=G)[:, :, 0:TC]

        def gb(ar, g, lo, hi):
            return ar[:, g * W1 + lo:g * W1 + hi]

        def vc8(name):
            i = VIDX[name]
            return vcs_sb[:, i * G:(i + 1) * G]

        def col(t, i):
            return t[:, i:i + 1]

        def bc(colap, n=TC):
            return colap.to_broadcast((128, n))

        ARN_BY_ID = {}
        E, Z, R = AR["aE"], AR["aZ"], AR["aR"]
        A, B = AR["aA"], AR["aB"]
        P1, P2 = AR["aP1"], AR["aP2"]
        X, Y, XB = AR["aX"], AR["aY"], AR["aXB"]
        for k, v in AR.items():
            ARN_BY_ID[id(v)] = k
        ARN_BY_ID[id(X2ALT)] = "x2alt"

        def XOF(j):
            return X if j % 2 == 0 else X2ALT

        def arn(ar):
            return ARN_BY_ID[id(ar)]

        dma("sync", vcs_sb[:, :], dram["vcs"].rearrange("p v g -> p (v g)"),
            wr=["vcs"])
        dma("sync", cfg_sb[:, :], dram["cfg"][:, :], wr=["cfg"])
        dma("sync", E[:, 0:G], dram["xbnd"][:, :], wr=["aE"])

        op("scalar", lambda e: e.activation(COLS["ew8"][:, :], vc8("w"),
                                            Act.Exp), rd=["vcs"], wr=["ew8"])
        op("scalar", lambda e: e.activation(COLS["emw8"][:, :], vc8("w"),
                                            Act.Exp, scale=-1.0),
           rd=["vcs"], wr=["emw8"])
        op("vector", lambda e: e.tensor_sub(COLS["tc1"][:, :], vc8("u"),
                                            vc8("w")), rd=["vcs"], wr=["tc1"])
        op("scalar", lambda e: e.activation(COLS["euw8"][:, :],
                                            COLS["tc1"][:, :], Act.Exp),
           rd=["tc1"], wr=["euw8"])
        op("vector", lambda e: e.memset(negb[:, :], NEGBIG), wr=["negb"])
        op("vector", lambda e: e.memset(eps1[:, :], EPS), wr=["eps1"])
        op("vector", lambda e: e.memset(onec[:, :], 1.0), wr=["onec"])
        op("vector", lambda e: e.memset(a_st[0][:, :], 0.0), wr=["a_st0"])
        op("vector", lambda e: e.memset(b_st[0][:, :], 0.0), wr=["b_st0"])
        op("vector", lambda e: e.memset(p_st[0][:, :], 0.0), wr=["p_st0"])
        op("vector", lambda e: e.tensor_copy(mask1[:, :], cfg_sb[:, 0:1]),
           rd=["cfg"], wr=["mask1"])
        for nm, src in (("rkm8", "rk"), ("rvm8", "rv"), ("rrm8", "rr"),
                        ("crkm8", "crk"), ("crrm8", "crr")):
            op("vector", lambda e, nm=nm, src=src: e.tensor_scalar(
                COLS[nm][:, :], vc8(src), 1.0, -1.0, Alu.mult, Alu.add),
               rd=["vcs"], wr=[nm])

        def row_lnstats(simple, n=TC):
            op("vector", lambda e: e.tensor_scalar(
                rowap("row_mu", n), rowap("row_s", n), 1.0 / C, None,
                Alu.mult), rd=[rown("row_s")], wr=[rown("row_mu")])
            op("vector", lambda e: e.tensor_scalar(
                rowap("row_var", n), rowap("row_q", n), 1.0 / C, None,
                Alu.mult), rd=[rown("row_q")], wr=[rown("row_var")])
            op("vector", lambda e: e.tensor_tensor(
                rowap("row_t", n), rowap("row_mu", n), rowap("row_mu", n),
                Alu.mult), rd=[rown("row_mu")], wr=[rown("row_t")])
            op("vector", lambda e: e.tensor_sub(
                rowap("row_var", n), rowap("row_var", n), rowap("row_t", n)),
               rd=[rown("row_var"), rown("row_t")], wr=[rown("row_var")])
            op("scalar", lambda e: e.activation(
                rowap("row_t", n), rowap("row_var", n), Act.Ln,
                bias=eps1[0:1, 0:1]),
               rd=[rown("row_var"), "eps1"], wr=[rown("row_t")])
            op("scalar", lambda e: e.activation(
                rowap("row_r0", n), rowap("row_t", n), Act.Exp, scale=-0.5),
               rd=[rown("row_t")], wr=[rown("row_r0")])
            if simple:
                return
            op("scalar", lambda e: e.activation(
                rowap("row_rho", n), rowap("row_t", n), Act.Exp, scale=-1.0),
               rd=[rown("row_t")], wr=[rown("row_rho")])
            op("vector", lambda e: e.tensor_tensor(
                rowap("row_rho", n), rowap("row_var", n),
                rowap("row_rho", n), Alu.mult),
               rd=[rown("row_var"), rown("row_rho")], wr=[rown("row_rho")])
            op("scalar", lambda e: e.activation(
                rowap("row_t", n), rowap("row_rho", n), Act.Ln,
                bias=eps1[0:1, 0:1]),
               rd=[rown("row_rho"), "eps1"], wr=[rown("row_t")])
            op("scalar", lambda e: e.activation(
                rowap("row_rho", n), rowap("row_t", n), Act.Exp, scale=-0.5),
               rd=[rown("row_t")], wr=[rown("row_rho")])
            op("vector", lambda e: e.tensor_tensor(
                rowap("row_rho", n), rowap("row_r0", n), rowap("row_rho", n),
                Alu.mult),
               rd=[rown("row_r0"), rown("row_rho")], wr=[rown("row_rho")])

        def mk_murho(rho_name, n=TC):
            op("vector", lambda e: e.tensor_tensor(
                rowap("row_mr", n), rowap("row_mu", n), rowap(rho_name, n),
                Alu.mult),
               rd=[rown("row_mu"), rown(rho_name)], wr=[rown("row_mr")])

        def bcasts(r1, r2, bi=4, bj=5):
            op("tensor", lambda e, bi=bi: e.matmul(
                PB[bi][:, :], onec[0:1, 0:1].to_broadcast((1, 128)),
                rowap(r1), start=True, stop=True),
               rd=[rown(r1), "onec"], wr=[f"pb{bi}"])
            op("tensor", lambda e, bj=bj: e.matmul(
                PB[bj][:, :], onec[0:1, 0:1].to_broadcast((1, 128)),
                rowap(r2), start=True, stop=True),
               rd=[rown(r2), "onec"], wr=[f"pb{bj}"])

        def bcast_scalar(rname, off, pbi):
            op("tensor", lambda e: e.matmul(
                PB[pbi][:, 0:1], onec[0:1, 0:1].to_broadcast((1, 128)),
                ROWB[0:1, RSLOT[rname] * TC + off:RSLOT[rname] * TC + off + 1],
                start=True, stop=True),
               rd=[rown(rname), "onec"], wr=[f"pb{pbi}"])

        def ln_stats(src_ar, body=False, sqar=None):
            sq = A if sqar is None else sqar
            srcv = sb_body(src_ar) if body else c3(src_ar)
            op("scalar", lambda e: e.activation(c3(sq), srcv, Act.Square),
               rd=[arn(src_ar)], wr=[arn(sq)])
            for g in range(G):
                s_ap = gb(src_ar, g, 1, W1) if body else cg(src_ar, g)
                op("tensor", lambda e, s=s_ap, g=g: e.matmul(
                    PB[6][0:1, :], onec[:, :], s, start=(g == 0),
                    stop=(g == 7)), rd=[arn(src_ar), "onec"], wr=["pb6"])
            for g in range(G):
                op("tensor", lambda e, g=g: e.matmul(
                    PB[7][0:1, :], onec[:, :], cg(sq, g), start=(g == 0),
                    stop=(g == 7)), rd=[arn(sq), "onec"], wr=["pb7"])
            op("vector", lambda e: e.tensor_copy(rowap("row_s"),
                                                 PB[6][0:1, :]),
               rd=["pb6"], wr=[rown("row_s")])
            op("vector", lambda e: e.tensor_copy(rowap("row_q"),
                                                 PB[7][0:1, :]),
               rd=["pb7"], wr=[rown("row_q")])

        def apply_ln(src_ar, dst_ap_nm, affine, src_body=False,
                     bi=4, bj=5):
            for g in range(G):
                s_ap = gb(src_ar, g, 1, W1) if src_body else cg(src_ar, g)
                d_ap, d_nm = dst_ap_nm(g)
                op("vector", lambda e, s=s_ap, d=d_ap, bi=bi: e.tensor_tensor(
                    d, s, PB[bi][:, :], Alu.mult),
                   rd=[arn(src_ar), f"pb{bi}"], wr=[d_nm])
                op("vector", lambda e, d=d_ap, bj=bj: e.tensor_sub(
                    d, d, PB[bj][:, :]), rd=[d_nm, f"pb{bj}"], wr=[d_nm])
                if affine:
                    gn, bn = affine
                    op("vector", lambda e, d=d_ap, g=g: e.tensor_scalar(
                        d, d, col(vc8(gn), g), col(vc8(bn), g),
                        Alu.mult, Alu.add), rd=[d_nm, "vcs"], wr=[d_nm])

        op("scalar", lambda e: e.square(Z[:, 0:G], E[:, 0:G]),
           rd=["aE"], wr=["aZ"])
        op("tensor", lambda e: e.matmul(PB[6][0:1, 0:G], onec[:, :],
                                        E[:, 0:G], start=True, stop=True),
           rd=["onec", "aE"], wr=["pb6"])
        op("tensor", lambda e: e.matmul(PB[7][0:1, 0:G], onec[:, :],
                                        Z[:, 0:G], start=True, stop=True),
           rd=["onec", "aZ"], wr=["pb7"])
        op("vector", lambda e: e.tensor_reduce(
            rowap("row_s", 1), PB[6][0:1, 0:G], AxX, Alu.add),
           rd=["pb6"], wr=[rown("row_s")])
        op("vector", lambda e: e.tensor_reduce(
            rowap("row_q", 1), PB[7][0:1, 0:G], AxX, Alu.add),
           rd=["pb7"], wr=[rown("row_q")])
        row_lnstats(simple=False, n=1)
        bcast_scalar("row_mu", 0, 4)
        bcast_scalar("row_rho", 0, 5)
        op("vector", lambda e: e.tensor_sub(
            Z[:, 0:G], E[:, 0:G], bc(PB[4][:, 0:1], G)),
           rd=["aE", "pb4"], wr=["aZ"])
        op("vector", lambda e: e.tensor_tensor(
            Z[:, 0:G], Z[:, 0:G], bc(PB[5][:, 0:1], G), Alu.mult),
           rd=["aZ", "pb5"], wr=["aZ"])
        op("vector", lambda e: e.tensor_tensor(
            Z[:, 0:G], Z[:, 0:G], vc8("g1"), Alu.mult),
           rd=["aZ", "vcs"], wr=["aZ"])
        op("vector", lambda e: e.tensor_add(Z[:, 0:G], Z[:, 0:G], vc8("b1")),
           rd=["aZ", "vcs"], wr=["aZ"])
        op("vector", lambda e: e.tensor_tensor(
            COLS["bnd_xa"][:, :], Z[:, 0:G], bc(mask1[:, :], G), Alu.mult),
           rd=["aZ", "mask1"], wr=["bnd_xa"])

        par = 0
        for nm_i, nm_o, ngrp, wid in (("wg", "wgbf", HGF, C),
                                      ("wu", "wubf", HGF, C),
                                      ("wd", "wdbf", G, HFF)):
            for t in range(ngrp):
                for pp in range(wid // 1024):
                    src, sname = (X, "aX") if par == 0 else (P2, "aP2")
                    dst, dname = (Y, "aY") if par == 0 else (XB, "aXB")
                    par ^= 1
                    i32v = src[:, 0:1024].bitcast(I32)
                    bfv = dst[:, 0:512].bitcast(BF16)[:, 0:1024]
                    dma("sync", i32v,
                        dram[nm_i][t][:, pp * 1024:(pp + 1) * 1024],
                        wr=[sname])
                    op("vector", lambda e, o=bfv, i=i32v: e.tensor_copy(o, i),
                       rd=[sname], wr=[dname])
                    sig("vector")
                    dma("gpsimd",
                        dram[nm_o][t][:, pp * 1024:(pp + 1) * 1024], bfv,
                        rd=[dname])

        def emit_wkv_chunk(j, Abuf, Bbuf, a_init, a_init_nm, b_init,
                           b_init_nm, a_out, a_out_nm, p_old, p_old_nm,
                           p_new, p_new_nm, b_out, b_out_nm, per_dg_bufs,
                           save_mini):
            for dg in range(G):
                aS, aB, a0ap, aE_ = Abuf(dg)
                bS, bB, b0ap, bE_ = Bbuf(dg)
                mx_ap, gn_ap, mx_nm, gn_nm = per_dg_bufs(dg)
                ai = a_init(dg)
                bi = b_init(dg)
                op("vector", lambda e, b0ap=b0ap, bi=bi: e.tensor_copy(
                    b0ap, bi), rd=[b_init_nm], wr=["aB"])
                op("vector", lambda e, a0ap=a0ap, ai=ai: e.tensor_copy(
                    a0ap, ai), rd=[a_init_nm], wr=["aA"])
                op("vector", lambda e, dg=dg, bB=bB, bi=bi:
                   e.tensor_tensor_scan(
                    bB, bc(col(COLS["ew8"], dg)), cg(E, dg), bi,
                    Alu.mult, Alu.add),
                   rd=["ew8", "aE", b_init_nm], wr=["aB"])
                op("vector", lambda e, dg=dg, mx_ap=mx_ap, bS=bS:
                   e.scalar_tensor_tensor(
                    mx_ap, cg(E, dg), col(COLS["emw8"], dg), bS,
                    Alu.mult, Alu.max),
                   rd=["aE", "emw8", "aB"], wr=[mx_nm])
                op("scalar", lambda e, mx_ap=mx_ap, gn_ap=gn_ap:
                   e.activation(gn_ap, mx_ap, Act.Ln),
                   rd=[mx_nm], wr=[gn_nm])
                op("scalar", lambda e, gn_ap=gn_ap: e.activation(
                    gn_ap, gn_ap, Act.Exp, scale=-1.0),
                   rd=[gn_nm], wr=[gn_nm])
                op("vector", lambda e, mx_ap=mx_ap, gn_ap=gn_ap, bB=bB:
                   e.tensor_tensor(mx_ap, bB, gn_ap, Alu.mult),
                   rd=["aB", gn_nm], wr=[mx_nm])
                op("vector", lambda e, dg=dg, mx_ap=mx_ap, gn_ap=gn_ap:
                   e.scalar_tensor_tensor(
                    gn_ap, mx_ap, col(COLS["emw8"], dg), cg(Z, dg),
                    Alu.mult, Alu.mult),
                   rd=[mx_nm, "emw8", "aZ"], wr=[gn_nm])
                op("vector", lambda e, aB=aB, mx_ap=mx_ap, gn_ap=gn_ap,
                   ai=ai: e.tensor_tensor_scan(
                    aB, mx_ap, gn_ap, ai, Alu.mult, Alu.add),
                   rd=[gn_nm, mx_nm, a_init_nm], wr=["aA"])
                if save_mini:
                    op("vector", lambda e, dg=dg, aS=aS: e.tensor_copy(
                        col(COLS["am8"], dg), aS[:, TC - 1:TC]),
                       rd=["aA"], wr=["am8"])
                    op("vector", lambda e, dg=dg, bS=bS: e.tensor_copy(
                        col(COLS["bm8"], dg), bS[:, TC - 1:TC]),
                       rd=["aB"], wr=["bm8"])
                op("vector", lambda e, dg=dg, bE_=bE_: e.reciprocal(
                    col(COLS["tc1"], dg), bE_), rd=["aB"], wr=["tc1"])
                op("vector", lambda e, dg=dg, aE_=aE_, ao=a_out(dg):
                   e.tensor_tensor(ao, aE_, col(COLS["tc1"], dg), Alu.mult),
                   rd=["aA", "tc1"], wr=[a_out_nm])
                op("scalar", lambda e, dg=dg, bE_=bE_: e.activation(
                    col(COLS["tc2"], dg), bE_, Act.Ln),
                   rd=["aB"], wr=["tc2"])
                op("vector", lambda e, dg=dg, po=p_new(dg), pi=p_old(dg):
                   e.tensor_add(po, pi, col(COLS["tc2"], dg)),
                   rd=[p_old_nm, "tc2"], wr=[p_new_nm])
            op("vector", lambda e, t=b_out: e.memset(t[:, :], 1.0),
               wr=[b_out_nm])

        barrier()


        def xload1(j):
            dma("sync", c3(XOF(j)), dram["xT"][:, :, j * TC:(j + 1) * TC],
                wr=[arn(XOF(j))])

        def V1(j):
            Xj = XOF(j)
            if j + 1 < NCH:
                xload1(j + 1)
            ln_stats(Xj)
            row_lnstats(simple=False)
            mk_murho("row_rho")
            bcasts("row_rho", "row_mr")
            op("vector", lambda e: e.tensor_tensor(
                rowap("row_rho"), rowap("row_mu"), rowap("row_r0"),
                Alu.mult),
               rd=[rown("row_mu"), rown("row_r0")], wr=[rown("row_rho")])
            dma("gpsimd", dram["sp_rows"][0:1, j * TC:(j + 1) * TC],
                rowap("row_rho"), rd=[rown("row_rho")])
            dma("gpsimd", dram["sp_rows"][1:2, j * TC:(j + 1) * TC],
                rowap("row_r0"), rd=[rown("row_r0")])
            op("vector", lambda e: e.tensor_copy(
                XB[:, 0:G * W1:W1], COLS["bnd_xa"][:, :]),
               rd=["bnd_xa"], wr=["aXB"])
            apply_ln(Xj, lambda g: (gb(XB, g, 1, W1), "aXB"),
                     affine=("g1", "b1"))
            op("vector", lambda e: e.tensor_copy(
                COLS["bnd_xa"][:, :], XB[:, TC:G * W1:W1]),
               rd=["aXB"], wr=["bnd_xa"])
        def V1b(j):
            op("vector", lambda e: e.tensor_sub(c3(P2), sb_body(XB),
                                                sb_shift(XB)),
               rd=["aXB"], wr=["aP2"])
            for g in range(G):
                op("vector", lambda e, g=g: e.scalar_tensor_tensor(
                    cg(Y, g), cg(P2, g), col(COLS["rkm8"], g),
                    gb(XB, g, 1, W1), Alu.mult, Alu.add),
                   rd=["aP2", "rkm8", "aXB"], wr=["aY"])
            for g in range(G):
                op("vector", lambda e, g=g: e.scalar_tensor_tensor(
                    cg(P1, g), cg(P2, g), col(COLS["rvm8"], g),
                    gb(XB, g, 1, W1), Alu.mult, Alu.add),
                   rd=["aP2", "rvm8", "aXB"], wr=["aP1"])
            for g in range(G):
                op("vector", lambda e, g=g: e.scalar_tensor_tensor(
                    cg(P2, g), cg(P2, g), col(COLS["rrm8"], g),
                    gb(XB, g, 1, W1), Alu.mult, Alu.add),
                   rd=["aP2", "rrm8", "aXB"], wr=["aP2"])

        def T1(j):
            for mixrole, mnm, wnm, dst, dstn, act in (
                    ("P2", "aP2", "wr", R, "aR", Act.Sigmoid),
                    ("Y", "aY", "wk", E, "aE", None),
                    ("P1", "aP1", "wv", Z, "aZ", None)):
                Mt = {"Y": Y, "P1": P1, "P2": P2}[mixrole]
                for dg in range(G):
                    slot = dg % 2
                    sl_ap = ring_a[:, slot * 1024:(slot + 1) * 1024]
                    dma("sync", sl_ap, dram[wnm][dg], wr=[f"ring{slot}"])
                    pb_i = dg % 4
                    for g in range(G):
                        op("tensor", lambda e, dg=dg, g=g, sl=sl_ap,
                           ms=Mt, pb_i=pb_i, mnm=mnm: e.matmul(
                               PB[pb_i][:, :],
                               sl[:, g * 128:(g + 1) * 128], cg(ms, g),
                               start=(g == 0), stop=(g == 7)),
                           rd=[f"ring{slot}", mnm], wr=[f"pb{pb_i}"])
                    if act is None:
                        op("scalar", lambda e, dg=dg, d=dst, pb_i=pb_i:
                           e.copy(cg(d, dg), PB[pb_i][:, :]),
                           rd=[f"pb{pb_i}"], wr=[dstn])
                    else:
                        op("scalar", lambda e, dg=dg, d=dst, pb_i=pb_i,
                           act=act: e.activation(cg(d, dg), PB[pb_i][:, :],
                                                 act),
                           rd=[f"pb{pb_i}"], wr=[dstn])

        def W1ph(j):
            pj, pn = j % 2, (j + 1) % 2
            Xj = XOF(j)
            op("vector", lambda e, s=p_st[pj], j=j: e.tensor_copy(
                p_sv[:, j * G:(j + 1) * G], s[:, :]),
               rd=[f"p_st{pj}"], wr=["p_sv"])
            op("vector", lambda e, p=p_st[pj]: e.tensor_sub(
                COLS["bias8"][:, :], vc8("w"), p[:, :]),
               rd=["vcs", f"p_st{pj}"], wr=["bias8"])
            for dg in range(G):
                op("scalar", lambda e, dg=dg: e.activation(
                    cg(E, dg), cg(E, dg), Act.Exp,
                    bias=col(COLS["bias8"], dg)),
                   rd=["aE", "bias8"], wr=["aE"])
            for dg in range(G):
                op("vector", lambda e, dg=dg: e.tensor_tensor(
                    cg(Z, dg), cg(E, dg), cg(Z, dg), Alu.mult),
                   rd=["aE", "aZ"], wr=["aZ"])
            emit_wkv_chunk(
                j,
                Abuf=lambda dg: (A[:, 0:TC], A[:, 1:W1], A[:, 0:1],
                                 A[:, TC:W1]),
                Bbuf=lambda dg: (B[:, 0:TC], B[:, 1:W1], B[:, 0:1],
                                 B[:, TC:W1]),
                a_init=lambda dg, s=a_st[pj]: col(s, dg),
                a_init_nm=f"a_st{pj}",
                b_init=lambda dg, s=b_st[pj]: col(s, dg),
                b_init_nm=f"b_st{pj}",
                a_out=lambda dg, s=a_st[pn]: col(s, dg),
                a_out_nm=f"a_st{pn}",
                p_old=lambda dg, s=p_st[pj]: col(s, dg),
                p_old_nm=f"p_st{pj}",
                p_new=lambda dg, s=p_st[pn]: col(s, dg),
                p_new_nm=f"p_st{pn}",
                b_out=b_st[pn], b_out_nm=f"b_st{pn}",
                per_dg_bufs=lambda dg: (A[:, W1:W1 + TC],
                                        A[:, 2 * W1:2 * W1 + TC],
                                        "aA", "aA"),
                save_mini=(j == NCH - 1))

            sig("vector")
            sig("scalar")
            dma("gpsimd", dram["sp_ekw"][:, :, j * TC:(j + 1) * TC], c3(E),
                rd=["aE"])
            dma("gpsimd", dram["sp_zv"][:, :, j * TC:(j + 1) * TC], c3(Z),
                rd=["aZ"])
            dma("gpsimd", dram["sp_sgr"][:, :, j * TC:(j + 1) * TC], c3(R),
                rd=["aR"])

            if j == NCH - 1:
                zl = Z[:, TC - 1:G * TC:TC]
                el = E[:, TC - 1:G * TC:TC]
                xl = Xj[:, TC - 1:G * TC:TC]
                rl = R[:, TC - 1:G * TC:TC]
                op("vector", lambda e, zl=zl: e.tensor_tensor(
                    COLS["tc3"][:, :], zl, COLS["euw8"][:, :], Alu.mult),
                   rd=["aZ", "euw8"], wr=["tc3"])
                op("vector", lambda e: e.tensor_add(
                    COLS["tc3"][:, :], COLS["tc3"][:, :],
                    COLS["am8"][:, :]), rd=["tc3", "am8"], wr=["tc3"])
                op("vector", lambda e, el=el: e.tensor_tensor(
                    COLS["tc4"][:, :], el, COLS["euw8"][:, :], Alu.mult),
                   rd=["aE", "euw8"], wr=["tc4"])
                op("vector", lambda e: e.tensor_add(
                    COLS["tc4"][:, :], COLS["tc4"][:, :],
                    COLS["bm8"][:, :]), rd=["tc4", "bm8"], wr=["tc4"])
                op("vector", lambda e: e.reciprocal(
                    COLS["tc1"][:, 0:G], COLS["tc4"][:, :]),
                   rd=["tc4"], wr=["tc1"])
                op("vector", lambda e: e.tensor_tensor(
                    COLS["ylst"][:, :], COLS["tc3"][:, :],
                    COLS["tc1"][:, 0:G], Alu.mult),
                   rd=["tc3", "tc1"], wr=["ylst"])
                bcast_scalar("row_mu", TC - 1, 4)
                bcast_scalar("row_r0", TC - 1, 5)
                op("vector", lambda e, xl=xl: e.tensor_sub(
                    COLS["tc2"][:, :], xl, bc(PB[4][:, 0:1], G)),
                   rd=[arn(Xj), "pb4"], wr=["tc2"])
                op("vector", lambda e: e.tensor_tensor(
                    COLS["tc2"][:, :], COLS["tc2"][:, :],
                    bc(PB[5][:, 0:1], G), Alu.mult),
                   rd=["tc2", "pb5"], wr=["tc2"])
                op("vector", lambda e, rl=rl: e.tensor_tensor(
                    COLS["tc1"][:, :], rl, COLS["ylst"][:, :], Alu.mult),
                   rd=["aR", "ylst"], wr=["tc1"])
                op("vector", lambda e: e.tensor_add(
                    COLS["x1l8"][:, :], COLS["tc2"][:, :],
                    COLS["tc1"][:, :]), rd=["tc2", "tc1"], wr=["x1l8"])

        xload1(0)
        V1(0)
        V1b(0)
        for j in range(NCH):
            T1(j)
            if j + 1 < NCH:
                V1(j + 1)
                V1b(j + 1)
            W1ph(j)

        pN = NCH % 2
        op("vector", lambda e: e.tensor_copy(ccs[:, 0:8], a_st[pN][:, :]),
           rd=[f"a_st{pN}"], wr=["ccs"])
        op("vector", lambda e: e.tensor_copy(ccs[:, 8:16], p_st[pN][:, :]),
           rd=[f"p_st{pN}"], wr=["ccs"])
        op("vector", lambda e: e.tensor_copy(ccs[:, 16:24],
                                             COLS["x1l8"][:, :]),
           rd=["x1l8"], wr=["ccs"])
        s_ccin, n_ccin = dma("gpsimd", dram["ccin"][:, :], ccs[:, :],
                             rd=["ccs"])
        _wait("gpsimd", s_ccin, n_ccin)
        sems["sCC"]["n"] += 1
        nCC = sems["sCC"]["n"]
        hCC = sems["sCC"]["h"]
        EMIT["gpsimd"].append(lambda e: e.collective_compute(
            "AllGather", Alu.bypass,
            replica_groups=[[0, 1], [2, 3], [4, 5], [6, 7]],
            ins=[dram["ccin"][:, :]], outs=[dram["ccout"][:, :, :]],
        ).then_inc(hCC, 1))
        EMIT["gpsimd"].append(lambda e, v=nCC: e.wait_ge(hCC, v))
        dma("gpsimd", ccs[:, :], dram["ccout"][0], wr=["ccs"])
        op("vector", lambda e: e.tensor_tensor(
            COLS["a0c"][:, :], ccs[:, 0:8], bc(mask1[:, :], G), Alu.mult),
           rd=["ccs", "mask1"], wr=["a0c"])
        op("vector", lambda e: e.tensor_tensor(
            COLS["p0c"][:, :], ccs[:, 8:16], bc(mask1[:, :], G), Alu.mult),
           rd=["ccs", "mask1"], wr=["p0c"])
        op("vector", lambda e: e.tensor_copy(a_st[0][:, :],
                                             COLS["a0c"][:, :]),
           rd=["a0c"], wr=["a_st0"])
        op("vector", lambda e: e.tensor_copy(p_st[0][:, :],
                                             COLS["p0c"][:, :]),
           rd=["p0c"], wr=["p_st0"])
        op("vector", lambda e: e.tensor_copy(b_st[0][:, :],
                                             bc(mask1[:, :], G)),
           rd=["mask1"], wr=["b_st0"])
        op("vector", lambda e: e.tensor_tensor(
            COLS["x1l8"][:, :], ccs[:, 16:24], bc(mask1[:, :], G), Alu.mult),
           rd=["ccs", "mask1"], wr=["x1l8"])
        op("scalar", lambda e: e.square(COLS["tc2"][:, :],
                                        COLS["x1l8"][:, :]),
           rd=["x1l8"], wr=["tc2"])
        op("tensor", lambda e: e.matmul(
            PB[6][0:1, 0:G], onec[:, :], COLS["x1l8"][:, :], start=True,
            stop=True), rd=["x1l8", "onec"], wr=["pb6"])
        op("tensor", lambda e: e.matmul(
            PB[7][0:1, 0:G], onec[:, :], COLS["tc2"][:, :], start=True,
            stop=True), rd=["tc2", "onec"], wr=["pb7"])
        op("vector", lambda e: e.tensor_reduce(
            rowap("row_s", 1), PB[6][0:1, 0:G], AxX, Alu.add),
           rd=["pb6"], wr=[rown("row_s")])
        op("vector", lambda e: e.tensor_reduce(
            rowap("row_q", 1), PB[7][0:1, 0:G], AxX, Alu.add),
           rd=["pb7"], wr=[rown("row_q")])
        row_lnstats(simple=True, n=1)
        bcast_scalar("row_mu", 0, 4)
        bcast_scalar("row_r0", 0, 5)
        op("vector", lambda e: e.tensor_sub(
            COLS["tc2"][:, :], COLS["x1l8"][:, :], bc(PB[4][:, 0:1], G)),
           rd=["x1l8", "pb4"], wr=["tc2"])
        op("vector", lambda e: e.tensor_tensor(
            COLS["tc2"][:, :], COLS["tc2"][:, :], bc(PB[5][:, 0:1], G),
            Alu.mult), rd=["tc2", "pb5"], wr=["tc2"])
        op("vector", lambda e: e.tensor_tensor(
            COLS["tc2"][:, :], COLS["tc2"][:, :], vc8("g2"), Alu.mult),
           rd=["tc2", "vcs"], wr=["tc2"])
        op("vector", lambda e: e.tensor_add(
            COLS["tc2"][:, :], COLS["tc2"][:, :], vc8("b2")),
           rd=["tc2", "vcs"], wr=["tc2"])
        op("vector", lambda e: e.tensor_tensor(
            COLS["bnd_xb"][:, :], COLS["tc2"][:, :], bc(mask1[:, :], G),
            Alu.mult), rd=["tc2", "mask1"], wr=["bnd_xb"])

        barrier()

        dma("sync", c3(XOF(0)), dram["xT"][:, :, 0:TC], wr=[arn(XOF(0))])
        dma("sync", rowap("row_mr"), dram["sp_rows"][0:1, 0:TC],
            wr=[rown("row_mr")])
        dma("sync", rowap("row_r0"), dram["sp_rows"][1:2, 0:TC],
            wr=[rown("row_r0")])
        for j in range(NCH):
            Xj = XOF(j)
            if j + 1 < NCH:
                dma("sync", c3(XOF(j + 1)),
                    dram["xT"][:, :, (j + 1) * TC:(j + 2) * TC],
                    wr=[arn(XOF(j + 1))])
            dma("gpsimd", c3(E), dram["sp_ekw"][:, :, j * TC:(j + 1) * TC],
                wr=["aE"])
            dma("gpsimd", c3(Z), dram["sp_zv"][:, :, j * TC:(j + 1) * TC],
                wr=["aZ"])
            dma("gpsimd", c3(R), dram["sp_sgr"][:, :, j * TC:(j + 1) * TC],
                wr=["aR"])
            bcasts("row_r0", "row_mr")
            for g in range(G):
                op("vector", lambda e, g=g, s=cg(Xj, g): e.tensor_tensor(
                    cg(XB, g), s, PB[4][:, :], Alu.mult),
                   rd=[arn(Xj), "pb4"], wr=["aXB"])
                op("vector", lambda e, g=g: e.tensor_sub(
                    cg(XB, g), cg(XB, g), PB[5][:, :]),
                   rd=["aXB", "pb5"], wr=["aXB"])
            pj2, pn2 = j % 2, (j + 1) % 2
            op("vector", lambda e, j=j, p=p_st[pj2]: e.tensor_sub(
                COLS["tc3"][:, :], p_sv[:, j * G:(j + 1) * G], p[:, :]),
               rd=["p_sv", f"p_st{pj2}"], wr=["tc3"])
            op("scalar", lambda e: e.activation(
                COLS["tc3"][:, :], COLS["tc3"][:, :], Act.Exp),
               rd=["tc3"], wr=["tc3"])
            for dg in range(G):
                op("vector", lambda e, dg=dg: e.tensor_scalar(
                    cg(E, dg), cg(E, dg), col(COLS["tc3"], dg), None,
                    Alu.mult), rd=["aE", "tc3"], wr=["aE"])
                op("vector", lambda e, dg=dg: e.tensor_scalar(
                    cg(Z, dg), cg(Z, dg), col(COLS["tc3"], dg), None,
                    Alu.mult), rd=["aZ", "tc3"], wr=["aZ"])
            emit_wkv_chunk(
                j,
                Abuf=lambda dg: (gb(A, dg, 0, TC), gb(A, dg, 1, W1),
                                 gb(A, dg, 0, 1), gb(A, dg, TC, W1)),
                Bbuf=lambda dg: (gb(B, dg, 0, TC), gb(B, dg, 1, W1),
                                 gb(B, dg, 0, 1), gb(B, dg, TC, W1)),
                a_init=lambda dg, s=a_st[pj2]: col(s, dg),
                a_init_nm=f"a_st{pj2}",
                b_init=lambda dg, s=b_st[pj2]: col(s, dg),
                b_init_nm=f"b_st{pj2}",
                a_out=lambda dg, s=a_st[pn2]: col(s, dg),
                a_out_nm=f"a_st{pn2}",
                p_old=lambda dg, s=p_st[pj2]: col(s, dg),
                p_old_nm=f"p_st{pj2}",
                p_new=lambda dg, s=p_st[pn2]: col(s, dg),
                p_new_nm=f"p_st{pn2}",
                b_out=b_st[pn2], b_out_nm=f"b_st{pn2}",
                per_dg_bufs=lambda dg: (cg(P1, 0), cg(P2, 0), "aP1", "aP2"),
                save_mini=False)
            for dg in range(G):
                op("vector", lambda e, dg=dg: e.scalar_tensor_tensor(
                    cg(P1, dg), cg(Z, dg), col(COLS["euw8"], dg),
                    gb(A, dg, 0, TC), Alu.mult, Alu.add),
                   rd=["aZ", "euw8", "aA"], wr=["aP1"])
                op("vector", lambda e, dg=dg: e.scalar_tensor_tensor(
                    cg(P2, dg), cg(E, dg), col(COLS["euw8"], dg),
                    gb(B, dg, 0, TC), Alu.mult, Alu.add),
                   rd=["aE", "euw8", "aB"], wr=["aP2"])
            Ac = A[:, 0:G * TC]
            Bc = B[:, 0:G * TC]
            op("scalar", lambda e: e.activation(Ac, c3(P2), Act.Ln),
               rd=["aP2"], wr=["aA"])
            op("scalar", lambda e: e.activation(Bc, Ac, Act.Exp,
                                                scale=-1.0),
               rd=["aA"], wr=["aB"])
            op("vector", lambda e: e.tensor_tensor(
                Ac, P1[:, 0:G * TC], Bc, Alu.mult),
               rd=["aP1", "aB"], wr=["aA"])
            op("vector", lambda e: e.tensor_tensor(
                Bc, Ac, R[:, 0:G * TC], Alu.mult),
               rd=["aA", "aR"], wr=["aB"])
            op("vector", lambda e: e.tensor_add(
                Y[:, 0:G * TC], XB[:, 0:G * TC], Bc),
               rd=["aXB", "aB"], wr=["aY"])

            ln_stats(Y)
            row_lnstats(simple=True)
            mk_murho("row_r0")
            bcasts("row_r0", "row_mr")
            if j + 1 < NCH:
                dma("sync", rowap("row_mr"),
                    dram["sp_rows"][0:1, (j + 1) * TC:(j + 2) * TC],
                    wr=[rown("row_mr")])
                dma("sync", rowap("row_r0"),
                    dram["sp_rows"][1:2, (j + 1) * TC:(j + 2) * TC],
                    wr=[rown("row_r0")])
            op("vector", lambda e: e.tensor_copy(
                XB[:, 0:G * W1:W1], COLS["bnd_xb"][:, :]),
               rd=["bnd_xb"], wr=["aXB"])
            apply_ln(Y, lambda g: (gb(XB, g, 1, W1), "aXB"),
                     affine=("g2", "b2"))
            op("vector", lambda e: e.tensor_copy(
                COLS["bnd_xb"][:, :], XB[:, TC:G * W1:W1]),
               rd=["aXB"], wr=["bnd_xb"])
            op("vector", lambda e: e.tensor_sub(c3(P1), sb_body(XB),
                                                sb_shift(XB)),
               rd=["aXB"], wr=["aP1"])
            for g in range(G):
                op("vector", lambda e, g=g: e.scalar_tensor_tensor(
                    cg(E, g), cg(P1, g), col(COLS["crkm8"], g),
                    gb(XB, g, 1, W1), Alu.mult, Alu.add),
                   rd=["aP1", "crkm8", "aXB"], wr=["aE"])
                op("vector", lambda e, g=g: e.scalar_tensor_tensor(
                    cg(Z, g), cg(P1, g), col(COLS["crrm8"], g),
                    gb(XB, g, 1, W1), Alu.mult, Alu.add),
                   rd=["aP1", "crrm8", "aXB"], wr=["aZ"])
            for dg in range(G):
                slot = dg % 2
                sl_ap = ring_a[:, slot * 1024:(slot + 1) * 1024]
                dma("sync", sl_ap, dram["wcr"][dg], wr=[f"ring{slot}"])
                for g in range(G):
                    op("tensor", lambda e, dg=dg, g=g, sl=sl_ap: e.matmul(
                        PB[dg % 2][:, :], sl[:, g * 128:(g + 1) * 128],
                        cg(Z, g), start=(g == 0), stop=(g == 7)),
                       rd=[f"ring{slot}", "aZ"], wr=[f"pb{dg % 2}"])
                op("scalar", lambda e, dg=dg: e.activation(
                    cg(R, dg), PB[dg % 2][:, :], Act.Sigmoid),
                   rd=[f"pb{dg % 2}"], wr=["aR"])
            kcslots = [A, B, Z, P1]

            def kc_ap(hg):
                return cg(kcslots[hg // 8], hg % 8)

            def kc_nm(hg):
                return arn(kcslots[hg // 8])

            for hg in range(HGK):
                slot = hg % 2
                sl_ap = ring_a[:, slot * 1024:(slot + 1) * 1024]
                dma("sync", sl_ap, dram["wck"][hg], wr=[f"ring{slot}"])
                for g in range(G):
                    op("tensor", lambda e, hg=hg, g=g, sl=sl_ap: e.matmul(
                        PB[2 + hg % 2][:, :], sl[:, g * 128:(g + 1) * 128],
                        cg(E, g), start=(g == 0), stop=(g == 7)),
                       rd=[f"ring{slot}", "aE"], wr=[f"pb{2 + hg % 2}"])
                op("scalar", lambda e, hg=hg: e.activation(
                    kc_ap(hg), PB[2 + hg % 2][:, :], Act.Relu),
                   rd=[f"pb{2 + hg % 2}"], wr=[kc_nm(hg)])
                op("vector", lambda e, hg=hg: e.tensor_tensor(
                    kc_ap(hg), kc_ap(hg), kc_ap(hg), Alu.mult),
                   rd=[kc_nm(hg)], wr=[kc_nm(hg)])
            for half in range(2):
                for dgl in range(4):
                    dg = half * 4 + dgl
                    for oc in range(4):
                        slot = oc % 2
                        sl_ap = ring_a[:, slot * 1024:(slot + 1) * 1024]
                        dma("sync", sl_ap,
                            dram["wcv"][dg][:, oc * 1024:(oc + 1) * 1024],
                            wr=[f"ring{slot}"])
                        for hh in range(8):
                            hg = oc * 8 + hh
                            op("tensor",
                               lambda e, hg=hg, hh=hh, sl=sl_ap, dgl=dgl:
                               e.matmul(
                                   PB[4 + dgl][:, :],
                                   sl[:, hh * 128:(hh + 1) * 128],
                                   kc_ap(hg), start=(hg == 0),
                                   stop=(hg == HGK - 1)),
                               rd=[f"ring{slot}", kc_nm(hg)],
                               wr=[f"pb{4 + dgl}"])
                    op("vector", lambda e, dg=dg, dgl=dgl: e.tensor_tensor(
                        cg(P2, dg), cg(R, dg), PB[4 + dgl][:, :], Alu.mult),
                       rd=["aR", f"pb{4 + dgl}"], wr=["aP2"])
            op("vector", lambda e: e.tensor_add(c3(XB), c3(Y), c3(P2)),
               rd=["aY", "aP2"], wr=["aXB"])
            op("vector", lambda e, s=c3(Xj): e.tensor_add(
                c3(XB), c3(XB), s), rd=["aXB", arn(Xj)], wr=["aXB"])
            sig("vector")
            dma("gpsimd", dram["sp_x2"][:, :, j * TC:(j + 1) * TC], c3(XB),
                rd=["aXB"])

        barrier()

        Ebf = E[:, 0:G * W1].bitcast(BF16)
        Abf = A[:, 0:G * W1].bitcast(BF16)

        def ybuf(jj):
            return (Ebf, "aE") if jj % 2 == 0 else (Abf, "aA")

        def y1part(jj, s, g):
            yb = Ebf if jj % 2 == 0 else Abf
            return yb[:, s * G * TC + g * TC:s * G * TC + (g + 1) * TC]

        def hh_ap(hg):
            if hg < 16:
                return B[:, 0:G * W1].bitcast(BF16)[
                    :, hg * TC:(hg + 1) * TC]
            return R[:, 0:G * W1].bitcast(BF16)[
                :, (hg - 16) * TC:(hg - 15) * TC]

        def hh_nm(hg):
            return "aB" if hg < 16 else "aR"

        def hl_ap(hg):
            if hg < 16:
                return Z[:, 0:G * W1].bitcast(BF16)[
                    :, hg * TC:(hg + 1) * TC]
            return XB[:, 0:G * W1].bitcast(BF16)[
                :, (hg - 16) * TC:(hg - 15) * TC]

        def hl_nm(hg):
            return "aZ" if hg < 16 else "aXB"

        def V2B(jj, phase):
            Xjj = XOF(jj)
            if phase == 0:
                ln_stats(Xjj, sqar=Y)
            elif phase == 1:
                row_lnstats(simple=True)
                mk_murho("row_r0")
                bcasts("row_r0", "row_mr", 6, 7)
            else:
                apply_ln(Xjj, lambda g: (cg(Y, g), "aY"), affine=None,
                         bi=6, bj=7)
                yb, ybnm = ybuf(jj)
                op("vector", lambda e, yb=yb, ybnm=ybnm: e.tensor_copy(
                    yb[:, 0:G * TC], Y[:, 0:G * TC]), rd=["aY"], wr=[ybnm])
                op("vector", lambda e, yb=yb, ybnm=ybnm: e.tensor_sub(
                    yb[:, G * TC:2 * G * TC], Y[:, 0:G * TC],
                    yb[:, 0:G * TC]), rd=["aY", ybnm], wr=[ybnm])

        dma("sync", c3(XOF(0)), dram["sp_x2"][:, :, 0:TC],
            wr=[arn(XOF(0))])
        V2B(0, 0)
        V2B(0, 1)
        V2B(0, 2)
        for j in range(NCH):
            Xj = XOF(j)
            ynm = ybuf(j)[1]
            if j + 1 < NCH:
                dma("sync", c3(XOF(j + 1)),
                    dram["sp_x2"][:, :, (j + 1) * TC:(j + 2) * TC],
                    wr=[arn(XOF(j + 1))])

            for hg in range(HGF):
                slot = hg % 2
                sl_g = ring_b[:, slot * 1024:(slot + 1) * 1024]
                sl_u = ring_b[:, (2 + slot) * 1024:(3 + slot) * 1024]
                dma("sync", sl_g, dram["wgbf"][hg], wr=[f"rbg{slot}"])
                dma("sync", sl_u, dram["wubf"][hg], wr=[f"rbu{slot}"])
                pg = 0 if hg % 2 == 0 else 2
                pu = 1 if hg % 2 == 0 else 3
                for s in range(2):
                    for g in range(G):
                        op("tensor", lambda e, s=s, g=g, sl=sl_g, pg=pg,
                           j=j: e.matmul(
                            PB[pg][:, :], sl[:, g * 128:(g + 1) * 128],
                            y1part(j, s, g), start=(s == 0 and g == 0),
                            stop=(s == 1 and g == 7)),
                           rd=[f"rbg{slot}", ynm], wr=[f"pb{pg}"])
                op("scalar", lambda e, pg=pg: e.activation(
                    cg(P1, 0), PB[pg][:, :], Act.Silu, scale=1.0 / 127.0),
                   rd=[f"pb{pg}"], wr=["aP1"])
                for s in range(2):
                    for g in range(G):
                        op("tensor", lambda e, s=s, g=g, sl=sl_u, pu=pu,
                           j=j: e.matmul(
                            PB[pu][:, :], sl[:, g * 128:(g + 1) * 128],
                            y1part(j, s, g), start=(s == 0 and g == 0),
                            stop=(s == 1 and g == 7)),
                           rd=[f"rbu{slot}", ynm], wr=[f"pb{pu}"])
                op("vector", lambda e, pu=pu: e.scalar_tensor_tensor(
                    cg(P1, 1), PB[pu][:, :], 1.0 / 127.0, cg(P1, 0),
                    Alu.mult, Alu.mult), rd=[f"pb{pu}", "aP1"], wr=["aP1"])
                op("vector", lambda e, hg=hg: e.tensor_copy(
                    hh_ap(hg), cg(P1, 1)), rd=["aP1"], wr=[hh_nm(hg)])
                op("vector", lambda e, hg=hg: e.tensor_sub(
                    hl_ap(hg), cg(P1, 1), hh_ap(hg)),
                   rd=["aP1", hh_nm(hg)], wr=[hl_nm(hg)])
                if j + 1 < NCH and hg in (1, 3, 5):
                    V2B(j + 1, (hg - 1) // 2)
            for dg in range(G):
                for pc in range(3):
                    slot = pc % 2
                    sl_d = ring_b[:, slot * 1024:(slot + 1) * 1024]
                    dma("sync", sl_d,
                        dram["wdbf"][dg][:, pc * 1024:(pc + 1) * 1024],
                        wr=[f"rbg{slot}"])
                    for hh in range(8):
                        hg = pc * 8 + hh
                        for s in range(2):
                            hap = hh_ap(hg) if s == 0 else hl_ap(hg)
                            hnm = hh_nm(hg) if s == 0 else hl_nm(hg)
                            op("tensor",
                               lambda e, hh=hh, s=s, h=hap, dg=dg, hg=hg,
                               sl=sl_d: e.matmul(
                                   PB[4 + dg % 2][:, :],
                                   sl[:, hh * 128:(hh + 1) * 128], h,
                                   start=(hg == 0 and s == 0),
                                   stop=(hg == HGF - 1 and s == 1)),
                               rd=[f"rbg{slot}", hnm],
                               wr=[f"pb{4 + dg % 2}"])
                op("vector", lambda e, dg=dg, s=cg(Xj, dg):
                   e.scalar_tensor_tensor(
                    cg(P2, dg), PB[4 + dg % 2][:, :], 1.0 / 127.0,
                    s, Alu.mult, Alu.add),
                   rd=[f"pb{4 + dg % 2}", arn(Xj)], wr=["aP2"])
            sig("vector")
            dma("gpsimd", dram["outT"][:, :, j * TC:(j + 1) * TC], c3(P2),
                rd=["aP2"])

        for i in range(DMA_ROT):
            nfin = sems[f"sG{i}"]["n"]
            hG = sems[f"sG{i}"]["h"]
            if nfin:
                EMIT["gpsimd"].append(
                    lambda e, v=nfin, h=hG: e.wait_ge(h, v))

        with nc.Block() as block:
            @block.sync
            def _(eng):
                for f in EMIT["sync"]:
                    f(eng)

            @block.vector
            def _(eng):
                for f in EMIT["vector"]:
                    f(eng)

            @block.scalar
            def _(eng):
                for f in EMIT["scalar"]:
                    f(eng)

            @block.tensor
            def _(eng):
                for f in EMIT["tensor"]:
                    f(eng)

            @block.gpsimd
            def _(eng):
                for f in EMIT["gpsimd"]:
                    f(eng)

    return nc



def _tile_kvr(w):
    return np.ascontiguousarray(
        w.reshape(G, 128, G, 128).transpose(0, 3, 2, 1).reshape(G, 128, C))


def _tile_ck(w):
    return np.ascontiguousarray(
        w.reshape(HGK, 128, G, 128).transpose(0, 3, 2, 1).reshape(
            HGK, 128, C))


def _tile_cv(w):
    return np.ascontiguousarray(
        w.reshape(G, 128, HGK, 128).transpose(0, 3, 2, 1).reshape(
            G, 128, HCK))


def _tile_ffn_up(w):
    return np.ascontiguousarray(
        w.reshape(HGF, 128, G, 128).transpose(0, 3, 2, 1).reshape(
            HGF, 128, C))


def _tile_ffn_dn(w):
    return np.ascontiguousarray(
        w.reshape(G, 128, HGF, 128).transpose(0, 3, 2, 1).reshape(
            G, 128, HFF))


def _vcs_pack(vs):
    a = np.stack([np.asarray(v, np.float32).reshape(G, 128).T for v in vs],
                 axis=1)
    return np.ascontiguousarray(a)


def make_in_maps(inp, TCORE):
    wk = _tile_kvr(np.asarray(inp["L0_attn_Wk"], np.float32))
    wv = _tile_kvr(np.asarray(inp["L0_attn_Wv"], np.float32))
    wr = _tile_kvr(np.asarray(inp["L0_attn_Wr"], np.float32))
    wcr = _tile_kvr(np.asarray(inp["L0_attn_Wcr"], np.float32))
    wckt = _tile_ck(np.asarray(inp["L0_attn_Wck"], np.float32))
    wcvt = _tile_cv(np.asarray(inp["L0_attn_Wcv"], np.float32))
    wg = _tile_ffn_up(np.asarray(inp["L0_ffn_w_gate"], np.int32))
    wu = _tile_ffn_up(np.asarray(inp["L0_ffn_w_up"], np.int32))
    wd = _tile_ffn_dn(np.asarray(inp["L0_ffn_w_down"], np.int32))
    vcs = _vcs_pack([inp["L0_attn_" + n] for n in (
        "w", "u", "rk", "rv", "rr", "crk", "crr", "ln1_g", "ln1_b",
        "ln2_g", "ln2_b")])
    x = np.asarray(inp["L0_attn_x"], np.float32)
    Bb = x.shape[0]
    in_maps = []
    for core in range(2 * Bb):
        b, h = core // 2, core % 2
        t0 = h * TCORE
        xs = x[b, t0:t0 + TCORE, :]
        xTc = np.ascontiguousarray(
            xs.T.reshape(G, 128, TCORE).transpose(1, 0, 2))
        if h == 0:
            xbnd = np.zeros((128, G), np.float32)
            maskv = 0.0
        else:
            xbnd = np.ascontiguousarray(x[b, t0 - 1, :].reshape(G, 128).T)
            maskv = 1.0
        cfgv = np.zeros((128, 4), np.float32)
        cfgv[:, 0] = maskv
        in_maps.append({
            "xT": xTc, "xbnd": xbnd, "wk": wk, "wv": wv, "wr": wr,
            "wcr": wcr, "wck": wckt, "wcv": wcvt, "wg": wg, "wu": wu,
            "wd": wd, "vcs": vcs, "cfg": cfgv,
        })
    return in_maps


def kernel(**inputs):
    x = np.asarray(inputs["L0_attn_x"], np.float32)
    Bb, Tt, Cc = x.shape
    assert Cc == C
    TCORE = Tt // 2
    nc = build_nc(TCORE=TCORE, TC=512)
    in_maps = make_in_maps(inputs, TCORE)
    res = run_bass_kernel_spmd(nc, in_maps, core_ids=list(range(8)))
    out = np.empty((Bb, Tt, Cc), np.float32)
    for core in range(8):
        b, h = core // 2, core % 2
        t0 = h * TCORE
        o = np.asarray(res.results[core]["outT"])
        out[b, t0:t0 + TCORE, :] = o.transpose(1, 0, 2).reshape(C, TCORE).T
    return out
